# version 18
# baseline (speedup 1.0000x reference)
import math
import numpy as np

import concourse.bass as bass
import concourse.mybir as mybir
from concourse.bass_utils import run_bass_kernel_spmd

F32 = mybir.dt.float32
BF16 = mybir.dt.bfloat16
I32 = mybir.dt.int32
ALU = mybir.AluOpType
ACT = mybir.ActivationFunctionType
AX = mybir.AxisListType
f32 = np.float32

D = 1024
G = 8
INNER = 4096
IG = 32
NCORES = 8

FULL_CFG = dict(T=2048, WARM=256, TAU=512, SL=256, debug=False)


_ISZ = {"float32": 4, "int32": 4, "uint32": 4, "bfloat16": 2,
        "float16": 2, "uint8": 1, "int8": 1}


def _isz(ap):
    d = str(ap.dtype)
    for k, v in _ISZ.items():
        if k in d:
            return v
    return 4


def _region(ap):
    name = ap.tensor.name
    apl = ap.ap
    sp = str(ap.space)
    isz = _isz(ap)
    if "DRAM" not in sp.upper():
        psz = ap.partition_size()
        step0 = apl[0][0] if apl else 0
        p0 = (ap.offset // step0) if step0 else 0
        f0 = ap.offset - p0 * step0
        span = 0
        for st_, cnt in apl[1:]:
            span += (cnt - 1) * abs(st_)
        return (name, p0, p0 + psz, f0 * isz, (f0 + span + 1) * isz)
    span = 0
    for st_, cnt in apl:
        span += (cnt - 1) * abs(st_)
    return (name, 0, 1, ap.offset * isz, (ap.offset + span + 1) * isz)


def _overlap(a, b):
    return (a[1] < b[2] and b[1] < a[2] and a[3] < b[4] and b[3] < a[4])


def _covers(a, b):
    return (a[1] <= b[1] and a[2] >= b[2] and a[3] <= b[3] and a[4] >= b[4])


class Prog:
    ENG = ("tensor", "vector", "scalar", "gpsimd", "sync")

    def __init__(self, nc, strict=False):
        self.nc = nc
        self.strict = strict
        self.steps = []
        self.mem_w = {}
        self.mem_r = {}
        self.deps = []
        self.psum_names = set()
        self.psum_last = {}

    def step(self, eng, emit, r=(), w=(), dma=False, cc=False, tag=""):
        sidx = len(self.steps)
        rds = [_region(a) for a in r]
        wrs = [_region(a) for a in w]
        dep = set()
        for reg in rds:
            for (oreg, oi) in self.mem_w.get(reg[0], ()):
                if _overlap(reg, oreg):
                    dep.add(oi)
        for reg in wrs:
            for (oreg, oi) in self.mem_w.get(reg[0], ()):
                if _overlap(reg, oreg):
                    dep.add(oi)
            for (oreg, oi) in self.mem_r.get(reg[0], ()):
                if _overlap(reg, oreg):
                    dep.add(oi)
        for reg in rds + wrs:
            if reg[0] in self.psum_names:
                prev = self.psum_last.get(reg[0])
                if prev is not None:
                    dep.add(prev)
                self.psum_last[reg[0]] = sidx
        for reg in rds:
            self.mem_r.setdefault(reg[0], []).append((reg, sidx))
        for reg in wrs:
            wl = self.mem_w.setdefault(reg[0], [])
            wl[:] = [e for e in wl if not _covers(reg, e[0])]
            wl.append((reg, sidx))
            rl = self.mem_r.get(reg[0])
            if rl:
                rl[:] = [e for e in rl if not _covers(reg, e[0])]
        self.steps.append(dict(eng=eng, emit=emit, dma=dma, cc=cc, tag=tag))
        self.deps.append(dep)
        return sidx

    def finalize(self, st):
        nc = self.nc
        n = len(self.steps)
        MAXPOOL = 12

        marked = [False] * n
        for dep in self.deps:
            for d in dep:
                marked[d] = True
        for i, s in enumerate(self.steps):
            if s["dma"] or s["cc"]:
                marked[i] = True

        key_of = [None] * n
        val_of = [0] * n
        ccnt = {}
        for i, s in enumerate(self.steps):
            if s["dma"] or s["cc"]:
                continue
            k = ("c", s["eng"])
            if marked[i]:
                ccnt[k] = ccnt.get(k, 0) + 1
            key_of[i] = k
            val_of[i] = ccnt.get(k, 0)

        know = {e: {} for e in self.ENG}
        kv_at = [None] * n
        plan = {e: [] for e in self.ENG}
        pools = {}
        cc_cnt = 0
        nwaits = 0
        for i, s in enumerate(self.steps):
            e = s["eng"]
            cur = know[e]
            waits = []

            def _join(src):
                for k2, v2 in kv_at[src].items():
                    if v2 > cur.get(k2, 0):
                        cur[k2] = v2

            for d in sorted(self.deps[i]):
                ds = self.steps[d]
                if (not self.strict and ds["eng"] == e and not ds["dma"]
                        and not ds["cc"]):
                    continue
                kk, v = key_of[d], val_of[d]
                if cur.get(kk, 0) >= v:
                    continue
                waits.append((kk, v))
                cur[kk] = v
                _join(d)

            if s["cc"]:
                cc_cnt += 1
                key_of[i], val_of[i] = ("cc", ""), cc_cnt
            elif s["dma"]:
                pool = pools.setdefault(e, [])
                slot = None
                for j, p in enumerate(pool):
                    if p["last"] == 0 or cur.get(("dm", e, j), 0) >= p["last"]:
                        slot = j
                        break
                if slot is None:
                    if len(pool) < MAXPOOL:
                        pool.append(dict(last=0, laststep=None))
                        slot = len(pool) - 1
                    else:
                        slot = min(range(len(pool)),
                                   key=lambda j: pool[j]["last"])
                p = pool[slot]
                kk = ("dm", e, slot)
                if p["last"] > 0:
                    waits.append((kk, p["last"]))
                    if cur.get(kk, 0) < p["last"]:
                        cur[kk] = p["last"]
                    if p["laststep"] is not None:
                        _join(p["laststep"])
                key_of[i], val_of[i] = kk, p["last"] + 16
                p["last"] += 16
                p["laststep"] = i

            myk = dict(cur)
            if marked[i]:
                myk[key_of[i]] = val_of[i]
                if not s["dma"] and not s["cc"] and not self.strict:
                    cur[key_of[i]] = val_of[i]
            kv_at[i] = myk
            if waits:
                wm = {}
                for kk, v in waits:
                    if v > wm.get(kk, -1):
                        wm[kk] = v
                waits = list(wm.items())
                nwaits += len(waits)
            plan[e].append((waits, i))

        self.dbg_plan, self.dbg_keyval = plan, (key_of, val_of)
        handles = {}
        for e in self.ENG:
            handles[("c", e)] = st.enter_context(nc.semaphore(f"se_{e}"))
        for e, pool in pools.items():
            for j in range(len(pool)):
                handles[("dm", e, j)] = st.enter_context(
                    nc.semaphore(f"sd_{e}{j}"))
        handles[("cc", "")] = st.enter_context(nc.semaphore("se_cc"))

        blk = st.enter_context(nc.Block())

        def mk_body(ename):
            def body(eng):
                for waits, i in plan[ename]:
                    for kk, v in waits:
                        eng.wait_ge(handles[kk], v)
                    inst = self.steps[i]["emit"](eng)
                    if marked[i]:
                        s = self.steps[i]
                        inst.then_inc(handles[key_of[i]],
                                      16 if s["dma"] else 1)
            return body

        blk.tensor(mk_body("tensor"))
        blk.vector(mk_body("vector"))
        blk.scalar(mk_body("scalar"))
        blk.gpsimd(mk_body("gpsimd"))
        blk.sync(mk_body("sync"))
        return dict(waits=nwaits, steps=n,
                    pools={e: len(p) for e, p in pools.items()})


def _mm(P, out, lhsT, rhs, start, stop, tag=""):
    P.step("tensor",
           lambda e, o=out, l=lhsT, r=rhs, a=start, b=stop:
           e.matmul(o, l, r, start=a, stop=b),
           r=[lhsT, rhs], w=[out], tag=tag)


def _tp(P, out, in_, eye, tag=""):
    P.step("tensor", lambda e, o=out, i=in_, y=eye: e.transpose(o, i, y),
           r=[in_, eye], w=[out], tag=tag)


def _tt(P, out, a, b, op, tag=""):
    P.step("vector", lambda e, o=out, x=a, y=b, p=op:
           e.tensor_tensor(out=o, in0=x, in1=y, op=p),
           r=[a, b], w=[out], tag=tag)


def _pyf(v):
    return float(v) if isinstance(v, np.floating) else v


def _ts1(P, out, a, s1, op, tag=""):
    s1 = _pyf(s1)
    rr = [a] + ([s1] if isinstance(s1, bass.AP) else [])
    P.step("vector", lambda e, o=out, x=a, u=s1, p=op:
           e.tensor_single_scalar(out=o, in_=x, scalar=u, op=p),
           r=rr, w=[out], tag=tag)


def _ts2(P, out, a, s1, s2, op0, op1, tag=""):
    s1, s2 = _pyf(s1), _pyf(s2)
    rr = [a] + ([s1] if isinstance(s1, bass.AP) else []) \
        + ([s2] if isinstance(s2, bass.AP) else [])
    P.step("vector", lambda e, o=out, x=a, u=s1, v=s2, p0=op0, p1=op1:
           e.tensor_scalar(out=o, in0=x, scalar1=u, scalar2=v, op0=p0, op1=p1),
           r=rr, w=[out], tag=tag)


def _stt(P, out, in0, scalar, in1, op0, op1, tag=""):
    scalar = _pyf(scalar)
    rr = [in0, in1] + ([scalar] if isinstance(scalar, bass.AP) else [])
    P.step("vector", lambda e, o=out, x=in0, s=scalar, y=in1, p0=op0, p1=op1:
           e.scalar_tensor_tensor(out=o, in0=x, scalar=s, in1=y, op0=p0, op1=p1),
           r=rr, w=[out], tag=tag)


def _act(P, out, in_, func, bias=None, scale=None, accum=None, tag=""):
    kw = {}
    rr = [in_]
    if bias is not None and scale is None:
        scale = f32(1.0)
    if bias is not None:
        kw["bias"] = bias
        if isinstance(bias, bass.AP):
            rr.append(bias)
    if scale is not None:
        kw["scale"] = scale
        if isinstance(scale, bass.AP):
            rr.append(scale)
    ww = [out]
    if accum is not None:
        kw["accum_out"] = accum
        ww.append(accum)
    P.step("scalar", lambda e, o=out, i=in_, fn=func, k=dict(kw):
           e.activation(out=o, in_=i, func=fn, **k),
           r=rr, w=ww, tag=tag)


def _copy(P, out, in_, eng="scalar", tag=""):
    if eng == "scalar":
        P.step(eng, lambda e, o=out, i=in_: e.copy(out=o, in_=i),
               r=[in_], w=[out], tag=tag)
    else:
        P.step(eng, lambda e, o=out, i=in_: e.tensor_copy(out=o, in_=i),
               r=[in_], w=[out], tag=tag)


def _red(P, out, in_, op, absval=False, tag=""):
    P.step("vector", lambda e, o=out, i=in_, p=op, a=absval:
           e.tensor_reduce(out=o, in_=i, axis=AX.X, op=p,
                           apply_absolute_value=a),
           r=[in_], w=[out], tag=tag)


def _recip(P, out, in_, tag=""):
    P.step("vector", lambda e, o=out, i=in_: e.reciprocal(out=o, in_=i),
           r=[in_], w=[out], tag=tag)


def _dma(P, out, in_, eng="sync", tag=""):
    P.step(eng, lambda e, o=out, i=in_:
           e.dma_start(out=o, in_=i, allow_slow_non_contiguous=True),
           r=[in_], w=[out], dma=True, tag=tag)


def _dmas(P, out, in_, engs, tag=""):
    n = len(engs)
    step = 128 // n
    for i, eng in enumerate(engs):
        p0, p1 = i * step, (i + 1) * step
        _dma(P, out[p0:p1], in_[p0:p1], eng=eng, tag=tag)


def _memset(P, ap, val, eng="gpsimd", tag=""):
    P.step(eng, lambda e, a=ap, v=val: e.memset(a, v), w=[ap], tag=tag)


def _scan(P, out, d0, d1, init, tag=""):
    rr = [d0, d1] + ([init] if isinstance(init, bass.AP) else [])
    P.step("vector", lambda e, o=out, a=d0, b=d1, i=init:
           e.tensor_tensor_scan(out=o, data0=a, data1=b, initial=i,
                                op0=ALU.mult, op1=ALU.add),
           r=rr, w=[out], tag=tag)


def build_program(cfg):
    T, WARM, TAU, SL = cfg["T"], cfg["WARM"], cfg["TAU"], cfg["SL"]
    NS = T // TAU
    NSL = T // SL
    NWSL = WARM // SL
    WP = WARM + 1
    assert WARM % SL == 0 and T % TAU == 0 and T % SL == 0 and T % 128 == 0
    assert G * TAU <= 4096 and SL + 1 <= TAU * 4
    RT = math.sqrt(float(NCORES * T))
    import contextlib
    stack = contextlib.ExitStack()

    strict = bool(cfg.get("strict", True))
    nc = bass.Bass("TRN2", target_bir_lowering=False,
                   detect_race_conditions=False)
    P = Prog(nc, strict=strict)

    di = lambda n, s, dt=F32: nc.dram_tensor(n, list(s), dt,
                                             kind="ExternalInput").ap()
    dint = lambda n, s, dt=F32: nc.dram_tensor(n, list(s), dt).ap()
    do = lambda n, s, dt=F32: nc.dram_tensor(n, list(s), dt,
                                             kind="ExternalOutput").ap()

    xs_h = di("xs", (T, D))
    wqkv_h = di("wqkv", (D, 3 * D))
    bqkv_h = di("bqkv", (3 * D,))
    bln_h = di("bln", (D,))
    SHW, SHD = D // NCORES, INNER // NCORES
    wlns_h = di("wlns", (SHW, D))
    g0s_h, u0s_h, d0s_h = di("g0s", (SHW, INNER)), di("u0s", (SHW, INNER)), \
        di("d0s", (SHD, D))
    g1s_h, u1s_h, d1s_h = di("g1s", (SHW, INNER)), di("u1s", (SHW, INNER)), \
        di("d1s", (SHD, D))
    l1s_h = di("l1s", (4 * SHW, D))
    td_h, tf_h = di("td", (D,)), di("tf", (D,))
    mk_h, mv_h, mr_h = di("mixk", (D,)), di("mixv", (D,)), di("mixr", (D,))
    wm_h = di("wmask", (1, 1))
    oh_h = di("onehot", (1, NCORES))
    out_h = do("out", (T, D))

    qkv16_h = dint("qkv16", (D, 3 * D), BF16)
    wln16_h = dint("wln16", (D, D), BF16)
    gq_h = [dint("gq0", (D, INNER), BF16), dint("gq1", (D, INNER), BF16)]
    uq_h = [dint("uq0", (D, INNER), BF16), dint("uq1", (D, INNER), BF16)]
    dq_h = [dint("dq0", (INNER, D), BF16), dint("dq1", (INNER, D), BF16)]
    l1w16_h = dint("l1w16", (4 * D, D), BF16)
    gq_s = [dint("gq0s", (SHW, INNER), BF16), dint("gq1s", (SHW, INNER),
                                                   BF16)]
    uq_s = [dint("uq0s", (SHW, INNER), BF16), dint("uq1s", (SHW, INNER),
                                                   BF16)]
    dq_s = [dint("dq0s", (SHD, D), BF16), dint("dq1s", (SHD, D), BF16)]
    wln_s16 = dint("wlns16", (SHW, D), BF16)
    l1_s16 = dint("l1s16", (4 * SHW, D), BF16)
    mw_in = dint("mwin", (1, 128))
    mw_out = dint("mwout", (1, 128))
    q2b_h = dint("q2bs", (D, T), BF16)
    xn1_h = dint("xn1s", (D, T), BF16)
    vs_h = dint("vs", (D, T), BF16)
    rows_h = dint("rows", (10, T))
    cc_in = dint("cc_in", (D + 1, D))
    cc_out = dint("cc_out", (D + 1, D))
    ag_in = dint("ag_in", (D, WP))
    ag_out = dint("ag_out", (NCORES * D, WP))

    dbg = {}
    if cfg["debug"]:
        dbg["xq"] = do("dbg_xq", (D, T), BF16)
        dbg["gq"] = do("dbg_gq", (D, INNER), BF16)
        dbg["xf"] = do("dbg_xf", (D, T))
        dbg["x1"] = do("dbg_x1", (D, T))
        dbg["kv"] = do("dbg_kv", (D, D))
        dbg["wkv"] = do("dbg_wkv", (D, T))

    sb = lambda n, s, dt=F32: stack.enter_context(
        nc.sbuf_tensor(n, list(s), dt))
    x = sb("x", (128, G * T))
    _bigsz = max(IG * TAU, 5 * D, G * D + 2 * G * TAU,
                 2 * G * SL + 2 * G * (SL + 1) + (G + 1) * (WARM + 1))
    big = sb("big", (128, _bigsz))
    wst = sb("wst", (128, 14336), BF16)
    bb = sb("bb", (128, 16384), BF16)
    msc = sb("msc", (128, 4 * TAU))
    pstg = sb("pstg", (128, 1024))
    cols = sb("cols", (128, 128))
    eye32 = sb("eye32", (128, 128))
    ones16 = sb("ones16", (128, 128), BF16)
    ones32 = sb("ones32", (128, 128))
    m32 = sb("m32", (128, 16))
    S1 = sb("scr1", (128, 512))
    S2 = sb("scr2", (128, 512))
    I1 = stack.enter_context(nc.sbuf_tensor("scri", [128, 512], I32))
    wprep32x = S1
    ps = [stack.enter_context(nc.psum_tensor(f"ps{i}", [128, 512], F32))
          for i in range(8)]
    for i in range(8):
        P.psum_names.add(f"ps{i}")

    C_BQ, C_BK, C_BV, C_BLN = 0, 8, 16, 24
    C_EW, C_EU, C_MK, C_MV, C_MR = 32, 40, 48, 56, 64
    C_PCOL, C_CSQ = 72, 80
    C_MASK, C_OH = 88, 89
    MW = dict(g0=97, u0=98, d0=99, g1=100, u1=101, d1=102)
    T0, T1, T2, T3, T4, T5 = 103, 104, 105, 106, 107, 108
    PP_AC0, PP_AC1, PP_PT0, PP_PT1, PP_PT2, PP_SCW, PP_HALF = \
        112, 113, 114, 115, 116, 117, 118

    col = lambda i, n=1: cols[:, i:i + n]

    B_H = 0
    B_KV = 0
    B_E, B_EV = 0, G * SL
    B_AB = 2 * G * SL
    B_BB = B_AB + G * (SL + 1)
    B_XW = B_BB + G * (SL + 1)
    B_CAND = B_XW + G * WP
    assert B_CAND + WP <= IG * TAU + 64

    A_XN, A_Q2, A_K2, A_V = 0, G * TAU, 2 * G * TAU, 3 * G * TAU
    F_XQ, F_XQ2 = 0, G * TAU
    B_Q2R, B_VR, B_VT = 0, G * TAU, 2 * G * TAU
    L_XN1 = 0
    L_DX = max(G * (SL + 1), G * WP)
    L_XK = L_DX + G * SL
    L_XV = L_XK + G * SL
    L_WKV = L_XV + G * SL
    L_SR = L_WKV + G * SL
    assert L_SR + G * SL <= 16384

    W_QKV = 0
    W_TB = 6144
    WP_SA, WP_SB, WP_Q = 8192, 9728, 11264
    WF_G, WF_U, WF_D = 0, 8192, 12288
    WB_LN = 0
    WL_K, WL_V, WL_O = 0, 4096, 8192

    wprep32 = wprep32x

    TAUB = TAU // 128
    M_T = lambda i: msc[:, i * TAU:(i + 1) * TAU]
    R_S = lambda i, n=None: msc[0:1, (3 - i) * TAU:(3 - i) * TAU + (n or TAU)]
    xg_sl = lambda g, s0, n: x[:, g * T + s0: g * T + s0 + n]


    def pmax_to_row(tile, width, rowdst, bank):
        nb = width // 128
        for b in range(nb):
            pt = ps[bank][:, 0:128]
            _tp(P, pt, tile[:, b * 128:(b + 1) * 128], eye32[:], tag="pm_t")
            _red(P, m32[:, b:b + 1], pt, ALU.max, tag="pm_r")
        dsc = rows_h[8:9, 0:width].rearrange("r (b p) -> p r b", p=128)
        _dma(P, dsc, m32[:, 0:nb], eng="gpsimd", tag="pm_d")
        _dma(P, rowdst, rows_h[8:9, 0:width], eng="gpsimd", tag="pm_d2")

    _memset(P, ones16[:], 1.0)
    _memset(P, ones32[:], 1.0)
    onesr32 = ones32[0:1, 0:128]
    it_r, it_c = msc[:, 0:128], msc[:, 128:256]
    P.step("gpsimd", lambda e, a=it_r: e.iota(
        a, [[1, 128]], channel_multiplier=0,
        allow_small_or_imprecise_dtypes=True), w=[it_r])
    P.step("gpsimd", lambda e, a=it_c: e.iota(
        a, [[0, 128]], channel_multiplier=1,
        allow_small_or_imprecise_dtypes=True), w=[it_c])
    _tt(P, eye32[:], it_r, it_c, ALU.is_equal)

    def load_colvec(c0, src_h, ncols=G):
        for c in range(ncols):
            v = src_h[c * 128:(c + 1) * 128].rearrange("(p o) -> p o", o=1)
            _dma(P, col(c0 + c, 1), v, eng="gpsimd", tag="colvec")

    load_colvec(C_BQ, bqkv_h, 24)
    load_colvec(C_BLN, bln_h)
    load_colvec(C_EW, td_h)
    load_colvec(C_EU, tf_h)
    load_colvec(C_MK, mk_h)
    load_colvec(C_MV, mv_h)
    load_colvec(C_MR, mr_h)
    _dma(P, cols[0:1, C_MASK:C_MASK + 1], wm_h[:], eng="gpsimd")
    _dma(P, cols[0:1, C_OH:C_OH + NCORES], oh_h[:], eng="gpsimd")

    _act(P, col(T0, G), col(C_EW, G), ACT.Exp)
    _act(P, col(C_EW, G), col(T0, G), ACT.Exp, scale=f32(-1.0))
    _act(P, col(C_EU, G), col(C_EU, G), ACT.Exp)

    _mm(P, ps[0][:, 0:1], onesr32, cols[0:1, C_MASK:C_MASK + 1],
        True, True)
    _copy(P, col(C_MASK), ps[0][:, 0:1])
    _mm(P, ps[0][:, 0:NCORES], onesr32, cols[0:1, C_OH:C_OH + NCORES],
        True, True)
    _copy(P, col(C_OH, NCORES), ps[0][:, 0:NCORES])

    for c0 in (C_BQ, C_BK):
        _red(P, col(T0), col(c0, G), ALU.add)
        _mm(P, ps[0][:, 0:1], ones32[:], col(T0), True, True)
        _ts1(P, col(T1), ps[0][:, 0:1], f32(1.0 / D), ALU.mult)
        _ts1(P, col(c0, G), col(c0, G), col(T1), ALU.subtract)

    pending = []
    pump_ct = dict(n=0)

    def pump(n=1):
        while n > 0 and pending:
            try:
                next(pending[0])
                pump_ct["n"] += 1
                n -= 1
            except StopIteration:
                pending.pop(0)

    def drain():
        while pending:
            pump(4096)

    slotsA = dict(
        bufs=[S1[:, 0:512], S2[:, 0:512], pstg[:, 0:512], pstg[:, 512:1024]],
        sA=[wst[:, WP_SA + j * 512:WP_SA + (j + 1) * 512] for j in range(3)],
        sB=[wst[:, WP_SB + j * 512:WP_SB + (j + 1) * 512] for j in range(3)],
        q16=[wst[:, WP_Q + j * 512:WP_Q + (j + 1) * 512] for j in range(6)],
        mwrt=None, mwv=None)
    slotsF = dict(
        bufs=[pstg[:, 0:512], pstg[:, 512:1024]],
        sA=[bb[:, 8192 + j * 512:8192 + (j + 1) * 512] for j in range(3)],
        sB=[bb[:, 9728 + j * 512:9728 + (j + 1) * 512] for j in range(3)],
        q16=[bb[:, 11264 + j * 512:11264 + (j + 1) * 512] for j in range(6)],
        mwrt=pstg[0:1, 512:640], mwv=pstg[0:1, 640:641])

    def prep_qkv16():
        buf = big[:, 2 * D:5 * D]
        buf16 = wst[:, 0:3 * D]
        for cg in range(G):
            _dmas(P, buf, wqkv_h[cg * 128:(cg + 1) * 128, :],
                  ["gpsimd", "scalar", "gpsimd", "scalar"], tag="qkvw")
            yield
            for c0 in (0, D):
                _red(P, col(T0), buf[:, c0:c0 + D], ALU.add)
                _ts1(P, col(T0), col(T0), f32(1.0 / D), ALU.mult)
                _ts1(P, buf16[:, c0:c0 + D], buf[:, c0:c0 + D], col(T0),
                     ALU.subtract, tag="qcenter")
                yield
            _copy(P, buf16[:, 2 * D:3 * D], buf[:, 2 * D:3 * D], eng="vector")
            _dmas(P, qkv16_h[cg * 128:(cg + 1) * 128, :], buf16,
                  ["gpsimd", "scalar"], tag="qkvw16")
            yield

    def prep_plain16(src_h, qsh_h, agout_h, nrowsh, slots):
        nsub = D // 512
        n = (nrowsh // 128) * nsub
        stile = lambda k: src_h[(k // nsub) * 128:(k // nsub + 1) * 128,
                                (k % nsub) * 512:(k % nsub + 1) * 512]
        dtile = lambda k: qsh_h[(k // nsub) * 128:(k // nsub + 1) * 128,
                                (k % nsub) * 512:(k % nsub + 1) * 512]
        NB = len(slots["bufs"])
        LA = NB - 1
        for k in range(n + LA + 3):
            if k < n:
                _dmas(P, slots["bufs"][k % NB], stile(k), ["sync", "sync"],
                      tag="pl16")
            j = k - LA
            if 0 <= j < n:
                _copy(P, slots["q16"][j % 6], slots["bufs"][j % NB],
                      eng="vector")
            j3 = k - LA - 3
            if 0 <= j3 < n:
                _dma(P, dtile(j3), slots["q16"][j3 % 6], tag="pl16o",
                     eng="scalar")
            yield
        P.step("gpsimd", lambda e: e.collective_compute(
            "AllGather", ALU.bypass, replica_groups=[list(range(NCORES))],
            ins=[qsh_h.opt()], outs=[agout_h.opt()]),
            r=[qsh_h], w=[agout_h], cc=True, tag="plag")
        yield

    MWIX = dict(g0=0, u0=1, d0=2, g1=3, u1=4, d1=5)

    def prep_stats(src_h, nrowsh, ncol, mwslot, tag, slots, first=False):
        nb = nrowsh // 128
        nsub = ncol // 512
        n = nb * nsub
        accs = [col(PP_AC0), col(PP_AC1)]
        parts = [col(PP_PT0), col(PP_PT1), col(PP_PT2)]
        stile = lambda k: src_h[(k // nsub) * 128:(k // nsub + 1) * 128,
                                (k % nsub) * 512:(k % nsub + 1) * 512]
        if first:
            zr = slots["mwrt"]
            _memset(P, zr, 0.0)
            _dma(P, mw_in[0:1, 0:128], zr, eng="gpsimd", tag="mwz")
            yield

        def acc_(k):
            j = k % 2
            if k < 2:
                _copy(P, accs[j], parts[k % 3], eng="gpsimd")
            else:
                P.step("gpsimd", lambda e, o=accs[j], i=parts[k % 3]:
                       e.tensor_tensor(out=o, in0=o, in1=i, op=ALU.add),
                       r=[accs[j], parts[k % 3]], w=[accs[j]],
                       tag=tag + "acc")

        NB = len(slots["bufs"])
        LA = NB - 1
        for k in range(n + LA + 2):
            if k < n:
                _dmas(P, slots["bufs"][k % NB], stile(k), ["sync", "sync"],
                      tag=tag + "s")
            j = k - LA
            if 0 <= j < n:
                _red(P, parts[j % 3], slots["bufs"][j % NB],
                     ALU.add, absval=True)
            if 0 <= k - LA - 2 < n:
                acc_(k - LA - 2)
            yield
        P.step("gpsimd", lambda e, o=accs[0], i=accs[1]:
               e.tensor_tensor(out=o, in0=o, in1=i, op=ALU.add),
               r=[accs[0], accs[1]], w=[accs[0]], tag=tag + "accf")
        ix = MWIX[mwslot]
        mwrow = rows_h[9:10, 0:128]
        _dma(P, mwrow.rearrange("r (b p) -> p r b", p=128), accs[0],
             eng="gpsimd", tag=tag + "mw1")
        rtmp, mwv = slots["mwrt"], slots["mwv"]
        _dma(P, rtmp, mwrow, eng="gpsimd", tag=tag + "mw2")
        _red(P, mwv, rtmp, ALU.add)
        _dma(P, mw_in[0:1, ix:ix + 1], mwv, eng="gpsimd", tag=tag + "mw3")
        yield

    def mw_allreduce():
        P.step("gpsimd", lambda e: e.collective_compute(
            "AllReduce", ALU.add, replica_groups=[list(range(NCORES))],
            ins=[mw_in.opt()], outs=[mw_out.opt()]),
            r=[mw_in], w=[mw_out], cc=True, tag="mwar")
        yield

    def prep_quant(src_h, qsh_h, agout_h, nrow_full, ncol, mwslot, tag,
                   slots):
        nb = (nrow_full // NCORES) // 128
        nsub = ncol // 512
        n = nb * nsub
        ix = MWIX[mwslot]
        rtmp, mwv = slots["mwrt"], slots["mwv"]
        _dma(P, mwv, mw_out[0:1, ix:ix + 1], eng="gpsimd", tag=tag + "mwl")
        _ts2(P, mwv, mwv, f32(1.0 / (nrow_full * ncol)), f32(1e-5),
             ALU.mult, ALU.max)
        _ts1(P, rtmp, onesr32, mwv, ALU.mult)
        mwrow = rows_h[9:10, 0:128]
        _dma(P, mwrow, rtmp, eng="gpsimd", tag=tag + "mw4")
        mw = col(MW[mwslot])
        _dma(P, mw, mwrow.rearrange("r (p o) -> p (r o)", o=1),
             eng="gpsimd", tag=tag + "mw5")
        scw, half = col(PP_SCW), col(PP_HALF)
        _recip(P, scw, mw)
        _ts1(P, half, mw, f32(0.5), ALU.mult)
        yield

        stile = lambda k: src_h[(k // nsub) * 128:(k // nsub + 1) * 128,
                                (k % nsub) * 512:(k % nsub + 1) * 512]

        def signs_(k):
            buf = slots["bufs"][k % len(slots["bufs"])]
            sA, sB = slots["sA"][k % 3], slots["sB"][k % 3]
            q16 = slots["q16"][k % 6]
            _act(P, sA, buf, ACT.Sign, bias=f32(-0.5), scale=scw, tag="sgA")
            _act(P, sB, buf, ACT.Sign, bias=f32(0.5), scale=scw, tag="sgB")
            aeng = "vector" if k % 2 == 0 else "gpsimd"
            P.step(aeng, lambda e, o=sA, i=sB:
                   e.tensor_tensor(out=o, in0=o, in1=i, op=ALU.add),
                   r=[sA, sB], w=[sA], tag="wqadd")
            P.step("vector", lambda e, o=q16, i=sA, h_=half:
                   e.tensor_single_scalar(out=o, in_=i, scalar=h_,
                                          op=ALU.mult),
                   r=[sA, half], w=[q16], tag="wq16")

        NB = len(slots["bufs"])
        LA = NB - 1
        for k in range(n + LA + 3):
            if k < n:
                _dmas(P, slots["bufs"][k % NB], stile(k), ["sync", "sync"],
                      tag=tag + "q")
            j = k - LA
            if 0 <= j < n:
                signs_(j)
            j3 = k - LA - 3
            if 0 <= j3 < n:
                _dma(P, qsh_h[(j3 // nsub) * 128:(j3 // nsub + 1) * 128,
                              (j3 % nsub) * 512:(j3 % nsub + 1) * 512],
                     slots["q16"][j3 % 6], tag=tag + "o", eng="scalar")
            yield
        P.step("gpsimd", lambda e: e.collective_compute(
            "AllGather", ALU.bypass, replica_groups=[list(range(NCORES))],
            ins=[qsh_h.opt()], outs=[agout_h.opt()]),
            r=[qsh_h], w=[agout_h], cc=True, tag=tag + "ag")
        yield

    pending.append(prep_qkv16())
    for tt in range(T // 128):
        lb = big[:, (tt % 2) * D:(tt % 2) * D + D]
        _dmas(P, lb, xs_h[tt * 128:(tt + 1) * 128, :], ["sync", "sync"],
              tag="xld")
        for g in range(G):
            pt = ps[7][:, 0:128]
            _tp(P, pt, lb[:, g * 128:(g + 1) * 128], eye32[:], tag="xtp")
            _copy(P, x[:, g * T + tt * 128: g * T + (tt + 1) * 128], pt)
        pump(2)
    pump(64)

    def ln_slice(s, rows_m, rows_r, rows_b, rows_a, rows_si):
        s0 = s * TAU
        rsl = slice(s0, s0 + TAU)
        xb16 = [bb[:, A_Q2 + g * TAU:A_Q2 + (g + 1) * TAU] for g in range(G)]
        sq16 = [bb[:, A_K2 + g * TAU:A_K2 + (g + 1) * TAU] for g in range(G)]
        for g in range(G):
            xg = xg_sl(g, s0, TAU)
            _copy(P, xb16[g], xg, eng="scalar", tag="xb16")
            _act(P, sq16[g], xg, ACT.Square, tag="xsq")
        pS1, pS2 = ps[0][:, 0:TAU], ps[1][:, 0:TAU]
        for g in range(G):
            _mm(P, pS1, ones16[:], xb16[g], g == 0, g == G - 1, tag="lnS1")
        for g in range(G):
            _mm(P, pS2, ones16[:], sq16[g], g == 0, g == G - 1, tag="lnS2")
        m_t, r_t, t3, t4 = M_T(0), M_T(1), M_T(2), M_T(3)
        _ts1(P, m_t, pS1, f32(1.0 / D), ALU.mult, tag="ln_m")
        _act(P, t3, pS1, ACT.Square, scale=f32(1.0 / D), tag="ln_msq")
        _ts2(P, r_t, pS2, f32(1.0 / D), f32(1e-5), ALU.mult, ALU.add)
        _tt(P, r_t, r_t, t3, ALU.subtract)
        _act(P, r_t, r_t, ACT.Sqrt)
        _recip(P, r_t, r_t)
        amax = t3
        for g in range(G):
            xg = xg_sl(g, s0, TAU)
            _tt(P, t4, xg, m_t, ALU.subtract, tag="am_d")
            _act(P, t4, t4, ACT.Abs, tag="am_abs")
            if g == 0:
                _copy(P, amax, t4, eng="vector", tag="am0")
            else:
                _tt(P, amax, amax, t4, ALU.max, tag="am")
        r_am = R_S(0)
        pmax_to_row(amax, TAU, r_am, 2)
        r0, r1 = R_S(0), R_S(1)
        _tt(P, r0, r_am, r_t[0:1, :], ALU.mult, tag="mxrow")
        _ts1(P, r1, r0, f32(1e-5), ALU.max)
        _recip(P, r0, r1)
        _ts1(P, r0, r0, f32(127.0), ALU.mult)
        _tt(P, r0, r_t[0:1, :], r0, ALU.mult)
        _dma(P, rows_h[rows_a:rows_a + 1, rsl], r0, eng="gpsimd")
        _tt(P, r0, m_t[0:1, :], r0, ALU.mult)
        _dma(P, rows_h[rows_b:rows_b + 1, rsl], r0, eng="gpsimd")
        _ts1(P, r1, r1, f32(1.0 / 127.0), ALU.mult)
        _dma(P, rows_h[rows_si:rows_si + 1, rsl], r1, eng="gpsimd")
        return m_t, r_t

    ROW_B0, ROW_A0, ROW_SI0 = 0, 1, 2
    ROW_M1, ROW_R1, ROW_B1, ROW_A1, ROW_SI1 = 3, 4, 5, 6, 7
    SP = cfg.get("stop_phase", 99)

    kvacc = lambda cg: big[:, B_KV + cg * D: B_KV + (cg + 1) * D]
    vstage = lambda g: big[:, 8192 + g * TAU:8192 + (g + 1) * TAU]
    kstage = lambda g: big[:, 8192 + (G + g) * TAU:8192 + (G + g + 1) * TAU]
    AP_ = cfg.get("a_parts", 99)
    slotsA["mwrt"] = msc[0:1, 3 * TAU:3 * TAU + 128]
    slotsA["mwv"] = msc[0:1, 2 * TAU:2 * TAU + 1]
    if SP >= 4:
        pending.append(prep_stats(g0s_h, SHW, INNER, "g0", "g0", slotsA,
                                  first=True))
        pending.append(prep_stats(u0s_h, SHW, INNER, "u0", "u0", slotsA))
        pending.append(prep_stats(d0s_h, SHD, D, "d0", "d0", slotsA))
    if SP >= 8:
        pending.append(prep_stats(g1s_h, SHW, INNER, "g1", "g1", slotsA))
        pending.append(prep_stats(u1s_h, SHW, INNER, "u1", "u1", slotsA))
        pending.append(prep_stats(d1s_h, SHD, D, "d1", "d1", slotsA))
    if SP >= 4:
        pending.append(mw_allreduce())
        pending.append(prep_quant(g0s_h, gq_s[0], gq_h[0], D, INNER,
                                  "g0", "g0", slotsA))
        pending.append(prep_quant(u0s_h, uq_s[0], uq_h[0], D, INNER,
                                  "u0", "u0", slotsA))
        pending.append(prep_quant(d0s_h, dq_s[0], dq_h[0], INNER, D,
                                  "d0", "d0", slotsA))
    if SP >= 6:
        pending.append(prep_plain16(wlns_h, wln_s16, wln16_h, SHW, slotsA))
    for s in range(NS if SP >= 2 else 0):
        s0 = s * TAU
        sl = slice(s0, s0 + TAU)
        m_t, r_t = ln_slice(s, None, None, ROW_B0, ROW_A0, ROW_SI0)
        pump(4)
        if AP_ < 2:
            continue
        t4 = M_T(3)
        xn16 = []
        for g in range(G):
            xg = xg_sl(g, s0, TAU)
            _tt(P, t4, xg, m_t, ALU.subtract, tag="xn_s")
            dst = bb[:, A_XN + g * TAU:A_XN + (g + 1) * TAU]
            _tt(P, dst, t4, r_t, ALU.mult, tag="xn_m")
            xn16.append(dst)
        if AP_ < 3:
            continue
        q2 = [bb[:, A_Q2 + g * TAU:A_Q2 + (g + 1) * TAU] for g in range(G)]
        k2 = [bb[:, A_K2 + g * TAU:A_K2 + (g + 1) * TAU] for g in range(G)]
        v16 = [bb[:, A_V + g * TAU:A_V + (g + 1) * TAU] for g in range(G)]
        OB = 3
        for oblk in range(24 // OB):
            w0 = W_QKV + (oblk % 2) * (OB * 128 * G)
            wb = wst[:, w0:w0 + OB * 128 * G]
            src3 = qkv16_h.rearrange("(cg p) o -> p cg o", p=128)[
                :, :, oblk * OB * 128:(oblk + 1) * OB * 128]
            _dmas(P, wb.rearrange("p (cg o) -> p cg o", cg=G), src3,
                  ["sync", "sync", "sync", "sync"], tag="qkvwld")
            for o in range(OB):
                oo = oblk * OB + o
                nmm = TAU // 512 if TAU > 512 else 1
                pb = ps[oo % 6][:, 0:TAU]
                for cg in range(G):
                    lt = wb[:, cg * OB * 128 + o * 128:
                            cg * OB * 128 + (o + 1) * 128]
                    _mm(P, pb, lt, xn16[cg], cg == 0, cg == G - 1, tag="qkv")
                g = oo % G
                if oo < 8:
                    _act(P, q2[g], pb, ACT.Square, bias=col(C_BQ + g), tag="q2")
                elif oo < 16:
                    _act(P, k2[g], pb, ACT.Square, bias=col(C_BK + g), tag="k2")
                else:
                    _act(P, vstage(g), pb, ACT.Identity, bias=col(C_BV + g),
                         tag="v32")
                    _copy(P, v16[g], vstage(g), eng="vector", tag="v16")
                pump(4)
            pump(4)
        if AP_ < 4:
            continue
        for g in range(G):
            _dma(P, vs_h[g * 128:(g + 1) * 128, sl], v16[g], eng="gpsimd",
                 tag="vst")
        sq = [bb[:, A_XN + g * TAU:A_XN + (g + 1) * TAU] for g in range(G)]
        pQ2, pQQ, pK2 = ps[6][:, 0:TAU], ps[7][:, 0:TAU], ps[0][:, 0:TAU]
        for g in range(G):
            _mm(P, pQ2, ones16[:], q2[g], g == 0, g == G - 1, tag="S_q2")
        for g in range(G):
            _act(P, sq[g], q2[g], ACT.Square, tag="q2sq")
        for g in range(G):
            _mm(P, pQQ, ones16[:], sq[g], g == 0, g == G - 1, tag="S_qq")
        for g in range(G):
            _mm(P, pK2, ones16[:], k2[g], g == 0, g == G - 1, tag="S_k2")
        dq_t, rk_t, t3 = M_T(0), M_T(1), M_T(2)
        _act(P, dq_t, pQQ, ACT.Sqrt, tag="r2")
        _ts2(P, t3, pQ2, f32(1e-12), f32(1e-19), ALU.mult, ALU.add)
        _tt(P, dq_t, dq_t, t3, ALU.max)
        _recip(P, dq_t, dq_t)
        _ts1(P, rk_t, pK2, f32(1e-7), ALU.add)
        _recip(P, rk_t, rk_t)
        for g in range(G):
            _tt(P, q2[g], q2[g], dq_t, ALU.mult, tag="q2b")
            _dma(P, q2b_h[g * 128:(g + 1) * 128, sl], q2[g], eng="gpsimd",
                 tag="q2bst")
            _tt(P, kstage(g), k2[g], rk_t, ALU.mult, tag="k2a")
        for g in range(G):
            _act(P, sq[g], kstage(g), ACT.Square, tag="k2asq")
            if s == 0:
                _red(P, col(C_CSQ + g), sq[g], ALU.add, tag="csq0")
            else:
                _red(P, col(T4), sq[g], ALU.add, tag="csq")
                _tt(P, col(C_CSQ + g), col(C_CSQ + g), col(T4), ALU.add)
        pump(6)
        if AP_ < 5:
            continue
        for tb in range(TAUB):
            tb0 = W_TB
            kT = wst[:, tb0:tb0 + 1024]
            vT = wst[:, tb0 + 1024:tb0 + 2048]
            for g in range(G):
                pt = ps[1][:, 0:128]
                _tp(P, pt, kstage(g)[:, tb * 128:(tb + 1) * 128], eye32[:],
                    tag="k2aT")
                _copy(P, kT[:, g * 128:(g + 1) * 128], pt)
                pt2 = ps[2][:, 0:128]
                _tp(P, pt2, vstage(g)[:, tb * 128:(tb + 1) * 128], eye32[:],
                    tag="vT")
                _copy(P, vT[:, g * 128:(g + 1) * 128], pt2)
                pump(4)
            glob_tb = s * TAUB + tb
            for cg in range(G):
                for half in range(2):
                    pkv = ps[3 + half][:, 0:512]
                    _mm(P, pkv, kT[:, cg * 128:(cg + 1) * 128],
                        vT[:, half * 512:(half + 1) * 512], True, True,
                        tag="kvmm")
                    dst = kvacc(cg)[:, half * 512:(half + 1) * 512]
                    if glob_tb == 0:
                        _copy(P, dst, pkv, tag="kva0")
                    else:
                        _tt(P, dst, dst, pkv, ALU.add, tag="kva")
            pump(9)

    drain()
    for cg in range(G if SP >= 3 else 0):
        _dma(P, cc_in[cg * 128:(cg + 1) * 128, :], kvacc(cg), eng="gpsimd",
             tag="ccin")
    _dma(P, cc_in[D:D + 1, :].rearrange("r (g p) -> p r g", p=128),
         col(C_CSQ, G), eng="gpsimd", tag="ccin2")
    if SP >= 3:
        P.step("gpsimd", lambda e: e.collective_compute(
            "AllReduce", ALU.add, replica_groups=[list(range(NCORES))],
            ins=[cc_in.opt()], outs=[cc_out.opt()]),
            r=[cc_in], w=[cc_out], cc=True, tag="allreduce")

    if SP >= 8:
        pending.append(prep_quant(g1s_h, gq_s[1], gq_h[1], D, INNER,
                                  "g1", "g1", slotsF))
        pending.append(prep_quant(u1s_h, uq_s[1], uq_h[1], D, INNER,
                                  "u1", "u1", slotsF))
        pending.append(prep_quant(d1s_h, dq_s[1], dq_h[1], INNER, D,
                                  "d1", "d1", slotsF))
    if SP >= 9:
        pending.append(prep_plain16(l1s_h, l1_s16, l1w16_h, 4 * SHW, slotsF))

    def ffn(gqh, uqh, dqh, mwd_col, rb, ra, rsi, dbgname=None):
        for s in range(NS):
            s0 = s * TAU
            a_t, b_t, si_t = M_T(0), M_T(1), M_T(2)
            w512 = min(512, TAU)
            for (ri, tt_) in ((ra, a_t), (rb, b_t), (rsi, si_t)):
                rr = R_S(0)
                _dma(P, rr, rows_h[ri:ri + 1, s0:s0 + TAU], tag="rowld")
                for b_ in range(TAU // w512):
                    pb = ps[0][:, 0:w512]
                    _mm(P, pb, onesr32,
                        rr[:, b_ * w512:(b_ + 1) * w512], True, True,
                        tag="rbc")
                    _copy(P, tt_[:, b_ * w512:(b_ + 1) * w512], pb)
            zbs = [S1[:, 0:TAU], pstg[:, 0:TAU]]
            r1fs = [S2[:, 0:TAU], pstg[:, TAU:2 * TAU]]
            r1i = I1[:, 0:TAU]
            xq16 = []
            for g in range(G):
                zb, r1f = zbs[g % 2], r1fs[g % 2]
                xg = xg_sl(g, s0, TAU)
                _tt(P, zb, xg, a_t, ALU.mult, tag="f_z")
                _tt(P, zb, zb, b_t, ALU.subtract)
                _act(P, r1i, zb, ACT.Identity, bias=f32(512.0), tag="f_cvt")
                _copy(P, r1f, r1i, eng="scalar", tag="f_cvb")
                _stt(P, zb, zb, 512.0, r1f, ALU.add, ALU.subtract, tag="f_d")
                _stt(P, zb, zb, 0.5, r1f, ALU.is_ge, ALU.add, tag="f_rq")
                dst = bb[:, F_XQ + g * TAU:F_XQ + (g + 1) * TAU]
                _stt(P, dst, zb, f32(-512.0), si_t, ALU.add, ALU.mult,
                     tag="xq16")
                if cfg["debug"] and dbgname is not None:
                    _dma(P, dbg[dbgname][g * 128:(g + 1) * 128, s0:s0 + TAU],
                         dst, eng="gpsimd", tag="dbgxq")
                xq16.append(dst)
            NIB = 4
            h_of = lambda i: big[:, B_H + i * TAU:B_H + (i + 1) * TAU]
            amax, gs = M_T(3), S1[:, 0:TAU]
            mneg = S2[:, 0:TAU]
            for ib in range(IG // NIB):
                gwb = wst[:, WF_G + (ib % 2) * 4096:
                          WF_G + (ib % 2) * 4096 + 4096]
                uwb = wst[:, WF_U:WF_U + 4096]
                gsrc = gqh.rearrange("(cg p) i -> p cg i", p=128)[
                    :, :, ib * NIB * 128:(ib + 1) * NIB * 128]
                usrc = uqh.rearrange("(cg p) i -> p cg i", p=128)[
                    :, :, ib * NIB * 128:(ib + 1) * NIB * 128]
                _dmas(P, gwb.rearrange("p (cg i) -> p cg i", cg=G), gsrc,
                      ["sync", "sync", "sync", "sync"], tag="gld")
                _dmas(P, uwb.rearrange("p (cg i) -> p cg i", cg=G), usrc,
                      ["gpsimd", "gpsimd", "gpsimd", "gpsimd"], tag="uld")
                pump(2)
                for ii in range(NIB):
                    i = ib * NIB + ii
                    pg = ps[ii][:, 0:TAU]
                    pu = ps[4 + ii][:, 0:TAU]
                    for cg in range(G):
                        lt = gwb[:, cg * NIB * 128 + ii * 128:
                                 cg * NIB * 128 + (ii + 1) * 128]
                        _mm(P, pg, lt, xq16[cg], cg == 0, cg == G - 1,
                            tag="gmm")
                    for cg in range(G):
                        lt = uwb[:, cg * NIB * 128 + ii * 128:
                                 cg * NIB * 128 + (ii + 1) * 128]
                        _mm(P, pu, lt, xq16[cg], cg == 0, cg == G - 1,
                            tag="umm")
                    _act(P, gs, pg, ACT.Silu, tag="silu")
                    _tt(P, h_of(i), gs, pu, ALU.mult, tag="h")
                    if i == 0:
                        _copy(P, amax, h_of(i), eng="vector", tag="am0")
                        _ts1(P, mneg, h_of(i), -1.0, ALU.mult, tag="amn0")
                    else:
                        _tt(P, amax, amax, h_of(i), ALU.max, tag="am")
                        _stt(P, mneg, h_of(i), -1.0, mneg, ALU.mult, ALU.max,
                             tag="amn")
                    pump(3)
            drain()
            _tt(P, amax, amax, mneg, ALU.max, tag="am_fin")
            r_am2 = R_S(0)
            pmax_to_row(amax, TAU, r_am2, 0)
            rc, rs = R_S(0), R_S(1)
            _ts1(P, rc, r_am2, f32(1e-5), ALU.max)
            _recip(P, rs, rc)
            _ts1(P, rs, rs, f32(127.0), ALU.mult)
            _ts1(P, rc, rc, f32(1.0 / 127.0), ALU.mult)
            sc2_t, ds2_t = M_T(2), M_T(3)
            for (rr, tt_) in ((rs, sc2_t), (rc, ds2_t)):
                for b_ in range(TAU // w512):
                    pb = ps[0][:, 0:w512]
                    _mm(P, pb, onesr32,
                        rr[:, b_ * w512:(b_ + 1) * w512], True, True,
                        tag="r2bc")
                    _copy(P, tt_[:, b_ * w512:(b_ + 1) * w512], pb)
            z2s = [M_T(0), M_T(1), pstg[:, 0:TAU]]
            r2fs = [S1[:, 0:TAU], S2[:, 0:TAU], pstg[:, TAU:2 * TAU]]

            def d_chain(i):
                par = i % 3
                z2, r2f = z2s[par], r2fs[par]
                r2i = I1[:, 0:TAU]
                _tt(P, z2, h_of(i), sc2_t, ALU.mult, tag="z2")
                _act(P, r2i, z2, ACT.Identity, bias=f32(512.0), tag="d_cvt")
                _copy(P, r2f, r2i, eng="scalar", tag="d_cvb")
                _stt(P, z2, z2, 512.0, r2f, ALU.add, ALU.subtract, tag="d_d")
                _stt(P, z2, z2, 0.5, r2f, ALU.is_ge, ALU.add, tag="d_rq")
                xq2 = bb[:, F_XQ2 + par * TAU:F_XQ2 + par * TAU + TAU]
                _ts1(P, xq2, z2, f32(-512.0), ALU.add, tag="xq2")

            def d_mm(i):
                xq2 = bb[:, F_XQ2 + (i % 3) * TAU:F_XQ2 + (i % 3) * TAU + TAU]
                dwb = wst[:, WF_D + (i % 2) * D:WF_D + (i % 2) * D + D]
                _dmas(P, dwb, dqh[i * 128:(i + 1) * 128, :],
                      ["sync", "gpsimd"], tag="dld")
                for og in range(G):
                    _mm(P, ps[og][:, 0:TAU],
                        dwb[:, og * 128:(og + 1) * 128], xq2,
                        i == 0, i == IG - 1, tag="dmm")

            for i in range(IG + 2):
                if i < IG:
                    d_chain(i)
                if i >= 2:
                    d_mm(i - 2)
            for og in range(G):
                ztmp = z2s[og % 2]
                _tt(P, ztmp, ps[og][:, 0:TAU], ds2_t, ALU.mult, tag="d_ds")
                xg = xg_sl(og, s0, TAU)
                _tt(P, xg, xg, ztmp, ALU.add, tag="d_add")

    if SP >= 5:
        ffn(gq_h[0], uq_h[0], dq_h[0], col(MW["d0"]), ROW_B0, ROW_A0,
            ROW_SI0, dbgname="xq" if cfg["debug"] else None)
    drain()

    if cfg["debug"]:
        for g in range(G):
            nch = min(T, 4 * D)
            b = big[:, 0:nch]
            _copy(P, b, x[:, g * T:g * T + nch], eng="vector")
            _dma(P, dbg["xf"][g * 128:(g + 1) * 128, 0:nch], b, eng="gpsimd")

    wln16 = wst[:, WB_LN:WB_LN + G * D]

    def kv16s(cg):
        if cg < 4:
            return wst[:, 8192 + cg * D:8192 + (cg + 1) * D]
        return bb[:, 12288 + (cg - 4) * D:12288 + (cg - 3) * D]

    dcol, csq = col(T5), col(T4)
    for cg in range(G if SP >= 6 else 0):
        kbuf = big[:, 0:D]
        _dma(P, kbuf, cc_out[cg * 128:(cg + 1) * 128, :], tag="kvld")
        _dma(P, csq, cc_out[D:D + 1, cg * 128:(cg + 1) * 128].rearrange(
            "r p -> p r"), tag="csqld")
        _act(P, dcol, csq, ACT.Sqrt)
        _ts1(P, dcol, dcol, f32(1e-12), ALU.max)
        _recip(P, dcol, dcol)
        _ts1(P, dcol, dcol, f32(1.0 / RT), ALU.mult)
        _ts1(P, kv16s(cg), kbuf, dcol, ALU.mult, tag="kv16")
    for cg in range(G if SP >= 6 else 0):
        _dma(P, wln16[:, cg * D:(cg + 1) * D],
             wln16_h[cg * 128:(cg + 1) * 128, :], tag="wlnld")
    if cfg["debug"]:
        for cg in range(G):
            kb = big[:, D:2 * D]
            _copy(P, kb, kv16s(cg), eng="vector")
            _dma(P, dbg["kv"][cg * 128:(cg + 1) * 128, :], kb, eng="gpsimd")

    for s in range(NS if SP >= 6 else 0):
        s0 = s * TAU
        sl = slice(s0, s0 + TAU)
        q2r = [bb[:, B_Q2R + g * TAU:B_Q2R + (g + 1) * TAU] for g in range(G)]
        vr = [bb[:, B_VR + g * TAU:B_VR + (g + 1) * TAU] for g in range(G)]
        vt = [bb[:, B_VT + g * TAU:B_VT + (g + 1) * TAU] for g in range(G)]
        for g in range(G):
            _dma(P, q2r[g], q2b_h[g * 128:(g + 1) * 128, sl], tag="q2bld")
            _dma(P, vr[g], vs_h[g * 128:(g + 1) * 128, sl], eng="gpsimd",
                 tag="vld")
        for dg in range(G):
            pb = ps[dg % 4][:, 0:TAU]
            for cg in range(G):
                _mm(P, pb, kv16s(cg)[:, dg * 128:(dg + 1) * 128],
                    q2r[cg], cg == 0, cg == G - 1, tag="t2mm")
            _tt(P, vt[dg], vr[dg], pb, ALU.add, tag="vt")
        for eg in range(G):
            pb = ps[4 + eg % 4][:, 0:TAU]
            for dg in range(G):
                _mm(P, pb,
                    wln16[:, dg * D + eg * 128:dg * D + (eg + 1) * 128],
                    vt[dg], dg == 0, dg == G - 1, tag="lnmm")
            xg = xg_sl(eg, s0, TAU)
            _stt(P, xg, pb, col(C_BLN + eg), xg, ALU.add, ALU.add,
                 tag="attn0+")

    if cfg["debug"]:
        for g in range(G):
            nch = min(T, IG * TAU)
            b = big[:, 0:nch]
            _copy(P, b, x[:, g * T:g * T + nch], eng="vector")
            _dma(P, dbg["x1"][g * 128:(g + 1) * 128, 0:nch], b, eng="gpsimd")

    for s in range(NS if SP >= 7 else 0):
        s0 = s * TAU
        sl = slice(s0, s0 + TAU)
        m_t, r_t = ln_slice(s, None, None, ROW_B1, ROW_A1, ROW_SI1)
        t4 = M_T(3)
        for g in range(G):
            xg = xg_sl(g, s0, TAU)
            _tt(P, t4, xg, m_t, ALU.subtract, tag="xn1_s")
            dst = bb[:, A_XN + g * TAU:A_XN + (g + 1) * TAU]
            _tt(P, dst, t4, r_t, ALU.mult, tag="xn1_m")
            _dma(P, xn1_h[g * 128:(g + 1) * 128, sl], dst, eng="gpsimd",
                 tag="xn1st")
    if SP >= 7:
        for g in range(G):
            _dma(P, ag_in[g * 128:(g + 1) * 128, :],
                 x[:, g * T + T - WP: g * T + T], eng="gpsimd", tag="agin")
        P.step("gpsimd", lambda e: e.collective_compute(
            "AllGather", ALU.bypass, replica_groups=[list(range(NCORES))],
            ins=[ag_in.opt()], outs=[ag_out.opt()]),
            r=[ag_in], w=[ag_out], cc=True, tag="allgather")

    if SP >= 8:
        ffn(gq_h[1], uq_h[1], dq_h[1], col(MW["d1"]), ROW_B1, ROW_A1,
            ROW_SI1)
    drain()

    xw = [big[:, B_XW + g * WP:B_XW + (g + 1) * WP] for g in range(G)]
    cand = big[:, B_CAND:B_CAND + WP]
    for g in range(G if SP >= 9 else 0):
        for j in range(NCORES):
            _dma(P, cand, ag_out[j * D + g * 128:j * D + (g + 1) * 128, :],
                 tag="agc")
            if j == 0:
                _ts1(P, xw[g], cand, col(C_OH + j), ALU.mult, tag="xw0")
            else:
                _stt(P, xw[g], cand, col(C_OH + j), xw[g], ALU.mult, ALU.add,
                     tag="xw+")

    xn1w = [bb[:, L_XN1 + g * WP:L_XN1 + (g + 1) * WP] for g in range(G)]
    xb16w = [bb[:, L_DX + g * WP:L_DX + (g + 1) * WP] for g in range(G)]
    sq16w = [bb[:, L_XV + g * WP:L_XV + (g + 1) * WP] for g in range(G)]
    if SP >= 9:
        pS1, pS2 = ps[0][:, 0:WP], ps[1][:, 0:WP]
        for g in range(G):
            _copy(P, xb16w[g], xw[g], eng="vector")
            _act(P, sq16w[g], xw[g], ACT.Square)
        for g in range(G):
            _mm(P, pS1, ones16[:], xb16w[g], g == 0, g == G - 1, tag="wS1")
        for g in range(G):
            _mm(P, pS2, ones16[:], sq16w[g], g == 0, g == G - 1, tag="wS2")
        m_t = msc[:, 0:WP]
        r_t = msc[:, WP:2 * WP]
        t3 = msc[:, 2 * WP:3 * WP]
        _ts1(P, m_t, pS1, f32(1.0 / D), ALU.mult)
        _act(P, t3, pS1, ACT.Square, scale=f32(1.0 / D))
        _ts2(P, r_t, pS2, f32(1.0 / D), f32(1e-5), ALU.mult, ALU.add)
        _tt(P, r_t, r_t, t3, ALU.subtract)
        _act(P, r_t, r_t, ACT.Sqrt)
        _recip(P, r_t, r_t)
        for g in range(G):
            _tt(P, t3, xw[g], m_t, ALU.subtract)
            _tt(P, xn1w[g], t3, r_t, ALU.mult, tag="xn1w")

    e_b = lambda g: big[:, B_E + g * SL:B_E + (g + 1) * SL]
    ev_b = lambda g: big[:, B_EV + g * SL:B_EV + (g + 1) * SL]
    Ab = lambda g: big[:, B_AB + g * (SL + 1):B_AB + (g + 1) * (SL + 1)]
    Bb = lambda g: big[:, B_BB + g * (SL + 1):B_BB + (g + 1) * (SL + 1)]
    for g in range(G if SP >= 10 else 0):
        _memset(P, Ab(g)[:, 0:1], 0.0)
        _memset(P, Bb(g)[:, 0:1], 0.0)

    kw = wst[:, WL_K:WL_K + 4096]
    vw = wst[:, WL_V:WL_V + 4096]
    ow = wst[:, WL_O:WL_O + 4096]

    def l1_proj(psums, wbuf, wrow0, xin, n, tag):
        for half in range(2):
            src = l1w16_h[wrow0 + half * 512:wrow0 + (half + 1) * 512, :]
            src = src.rearrange("(c p) e -> p c e", p=128)
            _dma(P, wbuf.rearrange("p (c e) -> p c e", c=4), src,
                 tag=tag + "w")
            for eg in range(G):
                for dc in range(4):
                    d = half * 4 + dc
                    lt = wbuf[:, dc * D + eg * 128:dc * D + (eg + 1) * 128]
                    _mm(P, psums[eg][:, 0:n], lt, xin[d], d == 0, d == 7,
                        tag=tag)

    total_sl = NWSL + NSL
    for isl in range(total_sl if SP >= 10 else 0):
        warm = isl < NWSL
        n = SL
        if warm:
            xs_sl = [xn1w[g][:, isl * SL:isl * SL + n + 1] for g in range(G)]
        else:
            s_own = isl - NWSL
            tok0 = s_own * SL
            xn1sl = [bb[:, L_XN1 + g * (SL + 1):L_XN1 + (g + 1) * (SL + 1)]
                     for g in range(G)]
            for g in range(G):
                _dma(P, xn1sl[g][:, 1:n + 1],
                     xn1_h[g * 128:(g + 1) * 128, tok0:tok0 + n], tag="xn1ld")
                _copy(P, xn1sl[g][:, 0:1], col(C_PCOL + g), eng="vector")
            xs_sl = xn1sl
        for g in range(G):
            if warm and isl == NWSL - 1:
                _ts1(P, col(C_PCOL + g), xs_sl[g][:, n:n + 1], col(C_MASK),
                     ALU.mult, tag="pcolm")
            else:
                _copy(P, col(C_PCOL + g), xs_sl[g][:, n:n + 1], eng="vector")
        dx = [bb[:, L_DX + g * SL:L_DX + (g + 1) * SL] for g in range(G)]
        xk = [bb[:, L_XK + g * SL:L_XK + (g + 1) * SL] for g in range(G)]
        xv = [bb[:, L_XV + g * SL:L_XV + (g + 1) * SL] for g in range(G)]
        for g in range(G):
            _tt(P, dx[g], xs_sl[g][:, 1:n + 1], xs_sl[g][:, 0:n],
                ALU.subtract, tag="dx")
            _stt(P, xk[g], dx[g], col(C_MK + g), xs_sl[g][:, 0:n], ALU.mult,
                 ALU.add, tag="xk")
            _stt(P, xv[g], dx[g], col(C_MV + g), xs_sl[g][:, 0:n], ALU.mult,
                 ALU.add, tag="xv")
        l1_proj(ps, kw, 0, xk, n, "kp")
        for g in range(G):
            _act(P, e_b(g)[:, 0:n], ps[g][:, 0:n], ACT.Exp, tag="e")
        l1_proj(ps, vw, D, xv, n, "vp")
        for g in range(G):
            _tt(P, ev_b(g)[:, 0:n], e_b(g)[:, 0:n], ps[g][:, 0:n], ALU.mult,
                tag="ev")
        if warm:
            for g in range(G):
                _ts1(P, e_b(g)[:, 0:n], e_b(g)[:, 0:n], col(C_MASK),
                     ALU.mult, tag="emask")
                _ts1(P, ev_b(g)[:, 0:n], ev_b(g)[:, 0:n], col(C_MASK),
                     ALU.mult, tag="evmask")
        for g in range(G):
            _scan(P, Ab(g)[:, 1:n + 1],
                  col(C_EW + g).broadcast_to([128, n]), ev_b(g)[:, 0:n],
                  Ab(g)[:, 0:1], tag="scanA")
            _scan(P, Bb(g)[:, 1:n + 1],
                  col(C_EW + g).broadcast_to([128, n]), e_b(g)[:, 0:n],
                  Bb(g)[:, 0:1], tag="scanB")
        if not warm:
            xr = xk
            for g in range(G):
                _stt(P, xr[g], dx[g], col(C_MR + g), xs_sl[g][:, 0:n],
                     ALU.mult, ALU.add, tag="xr")
            for g in range(G):
                _stt(P, ev_b(g)[:, 0:n], ev_b(g)[:, 0:n], col(C_EU + g),
                     Ab(g)[:, 0:n], ALU.mult, ALU.add, tag="num")
                _stt(P, e_b(g)[:, 0:n], e_b(g)[:, 0:n], col(C_EU + g),
                     Bb(g)[:, 0:n], ALU.mult, ALU.add, tag="den")
            wkv = [bb[:, L_WKV + g * SL:L_WKV + (g + 1) * SL]
                   for g in range(G)]
            for g in range(G):
                _recip(P, e_b(g)[:, 0:n], e_b(g)[:, 0:n], tag="rden")
                _tt(P, wkv[g], ev_b(g)[:, 0:n], e_b(g)[:, 0:n], ALU.mult,
                    tag="wkv")
            l1_proj(ps, ow, 2 * D, xr, n, "rp")
            sr = [bb[:, L_SR + g * SL:L_SR + (g + 1) * SL] for g in range(G)]
            for g in range(G):
                _act(P, sr[g], ps[g][:, 0:n], ACT.Sigmoid, tag="sr")
                _tt(P, wkv[g], wkv[g], sr[g], ALU.mult, tag="rwkv")
            if cfg["debug"]:
                for g in range(G):
                    bscr = msc[:, 0:n]
                    _copy(P, bscr, wkv[g], eng="vector")
                    _dma(P, dbg["wkv"][g * 128:(g + 1) * 128,
                                       tok0:tok0 + n], bscr, eng="gpsimd")
            l1_proj(ps, ow, 3 * D, wkv, n, "op")
            for eg in range(G):
                xg = xg_sl(eg, tok0, n)
                _tt(P, xg, xg, ps[eg][:, 0:n], ALU.add, tag="l1+")
        for g in range(G):
            _copy(P, Ab(g)[:, 0:1], Ab(g)[:, n:n + 1], eng="vector")
            _copy(P, Bb(g)[:, 0:1], Bb(g)[:, n:n + 1], eng="vector")

    for tt in range(T // 128):
        ob = big[:, (tt % 2) * D:(tt % 2) * D + D]
        for g in range(G):
            pt = ps[tt % 4][:, 0:128]
            _tp(P, pt, x[:, g * T + tt * 128:g * T + (tt + 1) * 128],
                eye32[:], tag="otp")
            _copy(P, ob[:, g * 128:(g + 1) * 128], pt)
        _dma(P, out_h[tt * 128:(tt + 1) * 128, :], ob, tag="ost")

    if cfg["debug"]:
        _dma(P, dbg["gq"], gq_h[0], eng="gpsimd", tag="dbggq")
    P.step("gpsimd", lambda e: e.nop(), r=[out_h], tag="fence")

    stats = P.finalize(stack)
    stack.close()
    build_program.lastP = P
    return nc, stats


def _layout_inputs(inputs, cfg):
    T = cfg["T"]
    x = np.ascontiguousarray(np.asarray(inputs["L0_attn_x"],
                                        dtype=np.float32)).reshape(-1, D)
    A = lambda k: np.ascontiguousarray(np.asarray(inputs[k], np.float32))
    AT = lambda k: np.ascontiguousarray(np.asarray(inputs[k], np.float32).T)
    base = dict(
        wqkv=A("L0_attn_Wqkv"), bqkv=A("L0_attn_bqkv"),
        bln=A("L0_attn_bln"),
        td=A("L1_attn_time_decay"), tf=A("L1_attn_time_first"),
        mixk=A("L1_attn_time_mix_k").reshape(-1),
        mixv=A("L1_attn_time_mix_v").reshape(-1),
        mixr=A("L1_attn_time_mix_r").reshape(-1),
    )
    wln = A("L0_attn_Wln")
    g0t, u0t, d0t = AT("L0_ffn_w_gate"), AT("L0_ffn_w_up"), \
        AT("L0_ffn_w_down")
    g1t, u1t, d1t = AT("L1_ffn_w_gate"), AT("L1_ffn_w_up"), \
        AT("L1_ffn_w_down")
    l1full = np.vstack([AT("L1_attn_Wk"), AT("L1_attn_Wv"),
                        AT("L1_attn_Wr"), AT("L1_attn_Wo")])
    SHW, SHD = D // NCORES, (4 * D) // NCORES
    in_maps = []
    for c in range(NCORES):
        m = dict(base)
        m["xs"] = np.ascontiguousarray(x[c * T:(c + 1) * T, :])
        m["wmask"] = np.array([[0.0 if c == 0 else 1.0]], np.float32)
        oh = np.zeros((1, NCORES), np.float32)
        oh[0, (c - 1) % NCORES] = 1.0
        m["onehot"] = oh
        rw = slice(c * SHW, (c + 1) * SHW)
        rd = slice(c * SHD, (c + 1) * SHD)
        m["wlns"] = np.ascontiguousarray(wln[rw])
        m["g0s"] = np.ascontiguousarray(g0t[rw])
        m["u0s"] = np.ascontiguousarray(u0t[rw])
        m["d0s"] = np.ascontiguousarray(d0t[rd])
        m["g1s"] = np.ascontiguousarray(g1t[rw])
        m["u1s"] = np.ascontiguousarray(u1t[rw])
        m["d1s"] = np.ascontiguousarray(d1t[rd])
        m["l1s"] = np.ascontiguousarray(l1full[rd])
        in_maps.append(m)
    return in_maps


def kernel(**inputs):
    cfg = dict(FULL_CFG)
    nc, _ = build_program(cfg)
    in_maps = _layout_inputs(inputs, cfg)
    res = run_bass_kernel_spmd(nc, in_maps, core_ids=list(range(NCORES)))
    T = cfg["T"]
    out = np.concatenate([np.asarray(r["out"]) for r in res.results], axis=0)
    return np.ascontiguousarray(out.reshape(1, NCORES * T, D),
                                dtype=np.float32)


# version 19
# speedup vs baseline: 1.0434x; 1.0434x over previous
import math
import numpy as np

import concourse.bass as bass
import concourse.mybir as mybir
from concourse.bass_utils import run_bass_kernel_spmd

F32 = mybir.dt.float32
BF16 = mybir.dt.bfloat16
I32 = mybir.dt.int32
ALU = mybir.AluOpType
ACT = mybir.ActivationFunctionType
AX = mybir.AxisListType
f32 = np.float32

D = 1024
G = 8
INNER = 4096
IG = 32
NCORES = 8

FULL_CFG = dict(T=2048, WARM=256, TAU=512, SL=256, debug=False)


_ISZ = {"float32": 4, "int32": 4, "uint32": 4, "bfloat16": 2,
        "float16": 2, "uint8": 1, "int8": 1}


def _isz(ap):
    d = str(ap.dtype)
    for k, v in _ISZ.items():
        if k in d:
            return v
    return 4


def _region(ap):
    name = ap.tensor.name
    apl = ap.ap
    sp = str(ap.space)
    isz = _isz(ap)
    if "DRAM" not in sp.upper():
        psz = ap.partition_size()
        step0 = apl[0][0] if apl else 0
        p0 = (ap.offset // step0) if step0 else 0
        f0 = ap.offset - p0 * step0
        span = 0
        for st_, cnt in apl[1:]:
            span += (cnt - 1) * abs(st_)
        return (name, p0, p0 + psz, f0 * isz, (f0 + span + 1) * isz)
    span = 0
    for st_, cnt in apl:
        span += (cnt - 1) * abs(st_)
    return (name, 0, 1, ap.offset * isz, (ap.offset + span + 1) * isz)


def _overlap(a, b):
    return (a[1] < b[2] and b[1] < a[2] and a[3] < b[4] and b[3] < a[4])


def _covers(a, b):
    return (a[1] <= b[1] and a[2] >= b[2] and a[3] <= b[3] and a[4] >= b[4])


class Prog:
    ENG = ("tensor", "vector", "scalar", "gpsimd", "sync")

    def __init__(self, nc, strict=False):
        self.nc = nc
        self.strict = strict
        self.steps = []
        self.mem_w = {}
        self.mem_r = {}
        self.deps = []
        self.psum_names = set()
        self.psum_last = {}

    def step(self, eng, emit, r=(), w=(), dma=False, cc=False, tag=""):
        sidx = len(self.steps)
        rds = [_region(a) for a in r]
        wrs = [_region(a) for a in w]
        dep = set()
        for reg in rds:
            for (oreg, oi) in self.mem_w.get(reg[0], ()):
                if _overlap(reg, oreg):
                    dep.add(oi)
        for reg in wrs:
            for (oreg, oi) in self.mem_w.get(reg[0], ()):
                if _overlap(reg, oreg):
                    dep.add(oi)
            for (oreg, oi) in self.mem_r.get(reg[0], ()):
                if _overlap(reg, oreg):
                    dep.add(oi)
        for reg in rds + wrs:
            if reg[0] in self.psum_names:
                prev = self.psum_last.get(reg[0])
                if prev is not None:
                    dep.add(prev)
                self.psum_last[reg[0]] = sidx
        for reg in rds:
            self.mem_r.setdefault(reg[0], []).append((reg, sidx))
        for reg in wrs:
            wl = self.mem_w.setdefault(reg[0], [])
            wl[:] = [e for e in wl if not _covers(reg, e[0])]
            wl.append((reg, sidx))
            rl = self.mem_r.get(reg[0])
            if rl:
                rl[:] = [e for e in rl if not _covers(reg, e[0])]
        self.steps.append(dict(eng=eng, emit=emit, dma=dma, cc=cc, tag=tag))
        self.deps.append(dep)
        return sidx

    def finalize(self, st):
        nc = self.nc
        n = len(self.steps)
        MAXPOOL = 12

        marked = [False] * n
        for dep in self.deps:
            for d in dep:
                marked[d] = True
        for i, s in enumerate(self.steps):
            if s["dma"] or s["cc"]:
                marked[i] = True

        key_of = [None] * n
        val_of = [0] * n
        ccnt = {}
        for i, s in enumerate(self.steps):
            if s["dma"] or s["cc"]:
                continue
            k = ("c", s["eng"])
            if marked[i]:
                ccnt[k] = ccnt.get(k, 0) + 1
            key_of[i] = k
            val_of[i] = ccnt.get(k, 0)

        know = {e: {} for e in self.ENG}
        kv_at = [None] * n
        plan = {e: [] for e in self.ENG}
        pools = {}
        cc_cnt = 0
        nwaits = 0
        for i, s in enumerate(self.steps):
            e = s["eng"]
            cur = know[e]
            waits = []

            def _join(src):
                for k2, v2 in kv_at[src].items():
                    if v2 > cur.get(k2, 0):
                        cur[k2] = v2

            for d in sorted(self.deps[i]):
                ds = self.steps[d]
                if (not self.strict and ds["eng"] == e and not ds["dma"]
                        and not ds["cc"]):
                    continue
                kk, v = key_of[d], val_of[d]
                if cur.get(kk, 0) >= v:
                    continue
                waits.append((kk, v))
                cur[kk] = v
                _join(d)

            if s["cc"]:
                cc_cnt += 1
                key_of[i], val_of[i] = ("cc", ""), cc_cnt
            elif s["dma"]:
                pool = pools.setdefault(e, [])
                slot = None
                for j, p in enumerate(pool):
                    if p["last"] == 0 or cur.get(("dm", e, j), 0) >= p["last"]:
                        slot = j
                        break
                if slot is None:
                    if len(pool) < MAXPOOL:
                        pool.append(dict(last=0, laststep=None))
                        slot = len(pool) - 1
                    else:
                        slot = min(range(len(pool)),
                                   key=lambda j: pool[j]["last"])
                p = pool[slot]
                kk = ("dm", e, slot)
                if p["last"] > 0:
                    waits.append((kk, p["last"]))
                    if cur.get(kk, 0) < p["last"]:
                        cur[kk] = p["last"]
                    if p["laststep"] is not None:
                        _join(p["laststep"])
                key_of[i], val_of[i] = kk, p["last"] + 16
                p["last"] += 16
                p["laststep"] = i

            myk = dict(cur)
            if marked[i]:
                myk[key_of[i]] = val_of[i]
                if not s["dma"] and not s["cc"] and not self.strict:
                    cur[key_of[i]] = val_of[i]
            kv_at[i] = myk
            if waits:
                wm = {}
                for kk, v in waits:
                    if v > wm.get(kk, -1):
                        wm[kk] = v
                waits = list(wm.items())
                nwaits += len(waits)
            plan[e].append((waits, i))

        self.dbg_plan, self.dbg_keyval = plan, (key_of, val_of)
        handles = {}
        for e in self.ENG:
            handles[("c", e)] = st.enter_context(nc.semaphore(f"se_{e}"))
        for e, pool in pools.items():
            for j in range(len(pool)):
                handles[("dm", e, j)] = st.enter_context(
                    nc.semaphore(f"sd_{e}{j}"))
        handles[("cc", "")] = st.enter_context(nc.semaphore("se_cc"))

        blk = st.enter_context(nc.Block())

        def mk_body(ename):
            def body(eng):
                for waits, i in plan[ename]:
                    for kk, v in waits:
                        eng.wait_ge(handles[kk], v)
                    inst = self.steps[i]["emit"](eng)
                    if marked[i]:
                        s = self.steps[i]
                        inst.then_inc(handles[key_of[i]],
                                      16 if s["dma"] else 1)
            return body

        blk.tensor(mk_body("tensor"))
        blk.vector(mk_body("vector"))
        blk.scalar(mk_body("scalar"))
        blk.gpsimd(mk_body("gpsimd"))
        blk.sync(mk_body("sync"))
        return dict(waits=nwaits, steps=n,
                    pools={e: len(p) for e, p in pools.items()})


def _mm(P, out, lhsT, rhs, start, stop, tag=""):
    P.step("tensor",
           lambda e, o=out, l=lhsT, r=rhs, a=start, b=stop:
           e.matmul(o, l, r, start=a, stop=b),
           r=[lhsT, rhs], w=[out], tag=tag)


def _tp(P, out, in_, eye, tag=""):
    P.step("tensor", lambda e, o=out, i=in_, y=eye: e.transpose(o, i, y),
           r=[in_, eye], w=[out], tag=tag)


def _tt(P, out, a, b, op, tag=""):
    P.step("vector", lambda e, o=out, x=a, y=b, p=op:
           e.tensor_tensor(out=o, in0=x, in1=y, op=p),
           r=[a, b], w=[out], tag=tag)


def _pyf(v):
    return float(v) if isinstance(v, np.floating) else v


def _ts1(P, out, a, s1, op, tag=""):
    s1 = _pyf(s1)
    rr = [a] + ([s1] if isinstance(s1, bass.AP) else [])
    P.step("vector", lambda e, o=out, x=a, u=s1, p=op:
           e.tensor_single_scalar(out=o, in_=x, scalar=u, op=p),
           r=rr, w=[out], tag=tag)


def _ts2(P, out, a, s1, s2, op0, op1, tag=""):
    s1, s2 = _pyf(s1), _pyf(s2)
    rr = [a] + ([s1] if isinstance(s1, bass.AP) else []) \
        + ([s2] if isinstance(s2, bass.AP) else [])
    P.step("vector", lambda e, o=out, x=a, u=s1, v=s2, p0=op0, p1=op1:
           e.tensor_scalar(out=o, in0=x, scalar1=u, scalar2=v, op0=p0, op1=p1),
           r=rr, w=[out], tag=tag)


def _stt(P, out, in0, scalar, in1, op0, op1, tag=""):
    scalar = _pyf(scalar)
    rr = [in0, in1] + ([scalar] if isinstance(scalar, bass.AP) else [])
    P.step("vector", lambda e, o=out, x=in0, s=scalar, y=in1, p0=op0, p1=op1:
           e.scalar_tensor_tensor(out=o, in0=x, scalar=s, in1=y, op0=p0, op1=p1),
           r=rr, w=[out], tag=tag)


def _act(P, out, in_, func, bias=None, scale=None, accum=None, tag=""):
    kw = {}
    rr = [in_]
    if bias is not None and scale is None:
        scale = f32(1.0)
    if bias is not None:
        kw["bias"] = bias
        if isinstance(bias, bass.AP):
            rr.append(bias)
    if scale is not None:
        kw["scale"] = scale
        if isinstance(scale, bass.AP):
            rr.append(scale)
    ww = [out]
    if accum is not None:
        kw["accum_out"] = accum
        ww.append(accum)
    P.step("scalar", lambda e, o=out, i=in_, fn=func, k=dict(kw):
           e.activation(out=o, in_=i, func=fn, **k),
           r=rr, w=ww, tag=tag)


def _copy(P, out, in_, eng="scalar", tag=""):
    if eng == "scalar":
        P.step(eng, lambda e, o=out, i=in_: e.copy(out=o, in_=i),
               r=[in_], w=[out], tag=tag)
    else:
        P.step(eng, lambda e, o=out, i=in_: e.tensor_copy(out=o, in_=i),
               r=[in_], w=[out], tag=tag)


def _red(P, out, in_, op, absval=False, tag=""):
    P.step("vector", lambda e, o=out, i=in_, p=op, a=absval:
           e.tensor_reduce(out=o, in_=i, axis=AX.X, op=p,
                           apply_absolute_value=a),
           r=[in_], w=[out], tag=tag)


def _recip(P, out, in_, tag=""):
    P.step("vector", lambda e, o=out, i=in_: e.reciprocal(out=o, in_=i),
           r=[in_], w=[out], tag=tag)


def _dma(P, out, in_, eng="sync", tag=""):
    P.step(eng, lambda e, o=out, i=in_:
           e.dma_start(out=o, in_=i, allow_slow_non_contiguous=True),
           r=[in_], w=[out], dma=True, tag=tag)


def _dmas(P, out, in_, engs, tag=""):
    n = len(engs)
    step = 128 // n
    for i, eng in enumerate(engs):
        p0, p1 = i * step, (i + 1) * step
        _dma(P, out[p0:p1], in_[p0:p1], eng=eng, tag=tag)


def _memset(P, ap, val, eng="gpsimd", tag=""):
    P.step(eng, lambda e, a=ap, v=val: e.memset(a, v), w=[ap], tag=tag)


def _scan(P, out, d0, d1, init, tag=""):
    rr = [d0, d1] + ([init] if isinstance(init, bass.AP) else [])
    P.step("vector", lambda e, o=out, a=d0, b=d1, i=init:
           e.tensor_tensor_scan(out=o, data0=a, data1=b, initial=i,
                                op0=ALU.mult, op1=ALU.add),
           r=rr, w=[out], tag=tag)


def build_program(cfg):
    T, WARM, TAU, SL = cfg["T"], cfg["WARM"], cfg["TAU"], cfg["SL"]
    NS = T // TAU
    NSL = T // SL
    NWSL = WARM // SL
    WP = WARM + 1
    assert WARM % SL == 0 and T % TAU == 0 and T % SL == 0 and T % 128 == 0
    assert G * TAU <= 4096 and SL + 1 <= TAU * 4
    RT = math.sqrt(float(NCORES * T))
    import contextlib
    stack = contextlib.ExitStack()

    strict = bool(cfg.get("strict", True))
    nc = bass.Bass("TRN2", target_bir_lowering=False,
                   detect_race_conditions=False)
    P = Prog(nc, strict=strict)

    di = lambda n, s, dt=F32: nc.dram_tensor(n, list(s), dt,
                                             kind="ExternalInput").ap()
    dint = lambda n, s, dt=F32: nc.dram_tensor(n, list(s), dt).ap()
    do = lambda n, s, dt=F32: nc.dram_tensor(n, list(s), dt,
                                             kind="ExternalOutput").ap()

    xs_h = di("xs", (T, D))
    wqkv_h = di("wqkv", (D, 3 * D))
    bqkv_h = di("bqkv", (3 * D,))
    bln_h = di("bln", (D,))
    SHW, SHD = D // NCORES, INNER // NCORES
    wlns_h = di("wlns", (SHW, D))
    g0s_h, u0s_h, d0s_h = di("g0s", (SHW, INNER)), di("u0s", (SHW, INNER)), \
        di("d0s", (SHD, D))
    g1s_h, u1s_h, d1s_h = di("g1s", (SHW, INNER)), di("u1s", (SHW, INNER)), \
        di("d1s", (SHD, D))
    l1s_h = di("l1s", (4 * SHW, D))
    td_h, tf_h = di("td", (D,)), di("tf", (D,))
    mk_h, mv_h, mr_h = di("mixk", (D,)), di("mixv", (D,)), di("mixr", (D,))
    wm_h = di("wmask", (1, 1))
    oh_h = di("onehot", (1, NCORES))
    out_h = do("out", (T, D))

    qkv16_h = dint("qkv16", (D, 3 * D), BF16)
    wln16_h = dint("wln16", (D, D), BF16)
    gq_h = [dint("gq0", (D, INNER), BF16), dint("gq1", (D, INNER), BF16)]
    uq_h = [dint("uq0", (D, INNER), BF16), dint("uq1", (D, INNER), BF16)]
    dq_h = [dint("dq0", (INNER, D), BF16), dint("dq1", (INNER, D), BF16)]
    l1w16_h = dint("l1w16", (4 * D, D), BF16)
    gq_s = [dint("gq0s", (SHW, INNER), BF16), dint("gq1s", (SHW, INNER),
                                                   BF16)]
    uq_s = [dint("uq0s", (SHW, INNER), BF16), dint("uq1s", (SHW, INNER),
                                                   BF16)]
    dq_s = [dint("dq0s", (SHD, D), BF16), dint("dq1s", (SHD, D), BF16)]
    wln_s16 = dint("wlns16", (SHW, D), BF16)
    l1_s16 = dint("l1s16", (4 * SHW, D), BF16)
    mw_in = dint("mwin", (1, 128))
    mw_out = dint("mwout", (1, 128))
    q2b_h = dint("q2bs", (D, T), BF16)
    xn1_h = dint("xn1s", (D, T), BF16)
    vs_h = dint("vs", (D, T), BF16)
    rows_h = dint("rows", (10, T))
    cc_in = dint("cc_in", (D + 1, D))
    cc_out = dint("cc_out", (D + 1, D))
    ag_in = dint("ag_in", (D, WP))
    ag_out = dint("ag_out", (NCORES * D, WP))

    dbg = {}
    if cfg["debug"]:
        dbg["xq"] = do("dbg_xq", (D, T), BF16)
        dbg["gq"] = do("dbg_gq", (D, INNER), BF16)
        dbg["xf"] = do("dbg_xf", (D, T))
        dbg["x1"] = do("dbg_x1", (D, T))
        dbg["kv"] = do("dbg_kv", (D, D))
        dbg["wkv"] = do("dbg_wkv", (D, T))

    sb = lambda n, s, dt=F32: stack.enter_context(
        nc.sbuf_tensor(n, list(s), dt))
    x = sb("x", (128, G * T))
    _bigsz = max(IG * TAU, 5 * D, G * D + 2 * G * TAU,
                 2 * G * SL + 2 * G * (SL + 1) + (G + 1) * (WARM + 1))
    big = sb("big", (128, _bigsz))
    wst = sb("wst", (128, 14336), BF16)
    bb = sb("bb", (128, 16384), BF16)
    msc = sb("msc", (128, 4 * TAU))
    pstg = sb("pstg", (128, 1024))
    cols = sb("cols", (128, 128))
    eye32 = sb("eye32", (128, 128))
    ones16 = sb("ones16", (128, 128), BF16)
    ones32 = sb("ones32", (128, 128))
    m32 = sb("m32", (128, 16))
    S1 = sb("scr1", (128, 512))
    S2 = sb("scr2", (128, 512))
    I1 = stack.enter_context(nc.sbuf_tensor("scri", [128, 512], I32))
    wprep32x = S1
    ps = [stack.enter_context(nc.psum_tensor(f"ps{i}", [128, 512], F32))
          for i in range(8)]
    for i in range(8):
        P.psum_names.add(f"ps{i}")

    C_BQ, C_BK, C_BV, C_BLN = 0, 8, 16, 24
    C_EW, C_EU, C_MK, C_MV, C_MR = 32, 40, 48, 56, 64
    C_PCOL, C_CSQ = 72, 80
    C_MASK, C_OH = 88, 89
    MW = dict(g0=97, u0=98, d0=99, g1=100, u1=101, d1=102)
    T0, T1, T2, T3, T4, T5 = 103, 104, 105, 106, 107, 108
    PP_AC0, PP_AC1, PP_PT0, PP_PT1, PP_PT2, PP_SCW, PP_HALF = \
        112, 113, 114, 115, 116, 117, 118

    col = lambda i, n=1: cols[:, i:i + n]

    B_H = 0
    B_KV = 0
    B_E, B_EV = 0, G * SL
    B_AB = 2 * G * SL
    B_BB = B_AB + G * (SL + 1)
    B_XW = B_BB + G * (SL + 1)
    B_CAND = B_XW + G * WP
    assert B_CAND + WP <= IG * TAU + 64

    A_XN, A_Q2, A_K2, A_V = 0, G * TAU, 2 * G * TAU, 3 * G * TAU
    F_XQ, F_XQ2 = 0, G * TAU
    B_Q2R, B_VR, B_VT = 0, G * TAU, 2 * G * TAU
    L_XN1 = 0
    L_DX = max(G * (SL + 1), G * WP)
    L_XK = L_DX + G * SL
    L_XV = L_XK + G * SL
    L_WKV = L_XV + G * SL
    L_SR = L_WKV + G * SL
    assert L_SR + G * SL <= 16384

    W_QKV = 0
    W_TB = 6144
    WP_SA, WP_SB, WP_Q = 8192, 9728, 11264
    WF_G, WF_U, WF_D = 0, 8192, 12288
    WB_LN = 0
    WL_K, WL_V, WL_O = 0, 4096, 8192

    wprep32 = wprep32x

    TAUB = TAU // 128
    M_T = lambda i: msc[:, i * TAU:(i + 1) * TAU]
    R_S = lambda i, n=None: msc[0:1, (3 - i) * TAU:(3 - i) * TAU + (n or TAU)]
    xg_sl = lambda g, s0, n: x[:, g * T + s0: g * T + s0 + n]


    def pmax_to_row(tile, width, rowdst, bank):
        nb = width // 128
        for b in range(nb):
            pt = ps[bank][:, 0:128]
            _tp(P, pt, tile[:, b * 128:(b + 1) * 128], eye32[:], tag="pm_t")
            _red(P, m32[:, b:b + 1], pt, ALU.max, tag="pm_r")
        dsc = rows_h[8:9, 0:width].rearrange("r (b p) -> p r b", p=128)
        _dma(P, dsc, m32[:, 0:nb], eng="gpsimd", tag="pm_d")
        _dma(P, rowdst, rows_h[8:9, 0:width], eng="gpsimd", tag="pm_d2")

    _memset(P, ones16[:], 1.0)
    _memset(P, ones32[:], 1.0)
    onesr32 = ones32[0:1, 0:128]
    it_r, it_c = msc[:, 0:128], msc[:, 128:256]
    P.step("gpsimd", lambda e, a=it_r: e.iota(
        a, [[1, 128]], channel_multiplier=0,
        allow_small_or_imprecise_dtypes=True), w=[it_r])
    P.step("gpsimd", lambda e, a=it_c: e.iota(
        a, [[0, 128]], channel_multiplier=1,
        allow_small_or_imprecise_dtypes=True), w=[it_c])
    _tt(P, eye32[:], it_r, it_c, ALU.is_equal)

    def load_colvec(c0, src_h, ncols=G):
        for c in range(ncols):
            v = src_h[c * 128:(c + 1) * 128].rearrange("(p o) -> p o", o=1)
            _dma(P, col(c0 + c, 1), v, eng="gpsimd", tag="colvec")

    load_colvec(C_BQ, bqkv_h, 24)
    load_colvec(C_BLN, bln_h)
    load_colvec(C_EW, td_h)
    load_colvec(C_EU, tf_h)
    load_colvec(C_MK, mk_h)
    load_colvec(C_MV, mv_h)
    load_colvec(C_MR, mr_h)
    _dma(P, cols[0:1, C_MASK:C_MASK + 1], wm_h[:], eng="gpsimd")
    _dma(P, cols[0:1, C_OH:C_OH + NCORES], oh_h[:], eng="gpsimd")

    _act(P, col(T0, G), col(C_EW, G), ACT.Exp)
    _act(P, col(C_EW, G), col(T0, G), ACT.Exp, scale=f32(-1.0))
    _act(P, col(C_EU, G), col(C_EU, G), ACT.Exp)

    _mm(P, ps[0][:, 0:1], onesr32, cols[0:1, C_MASK:C_MASK + 1],
        True, True)
    _copy(P, col(C_MASK), ps[0][:, 0:1])
    _mm(P, ps[0][:, 0:NCORES], onesr32, cols[0:1, C_OH:C_OH + NCORES],
        True, True)
    _copy(P, col(C_OH, NCORES), ps[0][:, 0:NCORES])

    for c0 in (C_BQ, C_BK):
        _red(P, col(T0), col(c0, G), ALU.add)
        _mm(P, ps[0][:, 0:1], ones32[:], col(T0), True, True)
        _ts1(P, col(T1), ps[0][:, 0:1], f32(1.0 / D), ALU.mult)
        _ts1(P, col(c0, G), col(c0, G), col(T1), ALU.subtract)

    pending = []
    pump_ct = dict(n=0)

    def pump(n=1):
        while n > 0 and pending:
            try:
                next(pending[0])
                pump_ct["n"] += 1
                n -= 1
            except StopIteration:
                pending.pop(0)

    def drain():
        while pending:
            pump(4096)

    slotsA = dict(
        bufs=[S1[:, 0:512], S2[:, 0:512], pstg[:, 0:512], pstg[:, 512:1024]],
        sA=[wst[:, WP_SA + j * 512:WP_SA + (j + 1) * 512] for j in range(3)],
        sB=[wst[:, WP_SB + j * 512:WP_SB + (j + 1) * 512] for j in range(3)],
        q16=[wst[:, WP_Q + j * 512:WP_Q + (j + 1) * 512] for j in range(6)],
        mwrt=None, mwv=None)
    slotsF = dict(
        bufs=[pstg[:, 0:512], pstg[:, 512:1024]],
        sA=[bb[:, 8192 + j * 512:8192 + (j + 1) * 512] for j in range(3)],
        sB=[bb[:, 9728 + j * 512:9728 + (j + 1) * 512] for j in range(3)],
        q16=[bb[:, 11264 + j * 512:11264 + (j + 1) * 512] for j in range(6)],
        mwrt=pstg[0:1, 512:640], mwv=pstg[0:1, 640:641])

    def prep_qkv16():
        buf = big[:, 2 * D:5 * D]
        buf16 = wst[:, 0:3 * D]
        for cg in range(G):
            _dmas(P, buf, wqkv_h[cg * 128:(cg + 1) * 128, :],
                  ["gpsimd", "scalar", "gpsimd", "scalar"], tag="qkvw")
            yield
            for c0 in (0, D):
                _red(P, col(T0), buf[:, c0:c0 + D], ALU.add)
                _ts1(P, col(T0), col(T0), f32(1.0 / D), ALU.mult)
                _ts1(P, buf16[:, c0:c0 + D], buf[:, c0:c0 + D], col(T0),
                     ALU.subtract, tag="qcenter")
                yield
            _copy(P, buf16[:, 2 * D:3 * D], buf[:, 2 * D:3 * D], eng="vector")
            _dmas(P, qkv16_h[cg * 128:(cg + 1) * 128, :], buf16,
                  ["gpsimd", "scalar"], tag="qkvw16")
            yield

    def prep_plain16(src_h, qsh_h, agout_h, nrowsh, slots):
        nsub = D // 512
        n = (nrowsh // 128) * nsub
        stile = lambda k: src_h[(k // nsub) * 128:(k // nsub + 1) * 128,
                                (k % nsub) * 512:(k % nsub + 1) * 512]
        dtile = lambda k: qsh_h[(k // nsub) * 128:(k // nsub + 1) * 128,
                                (k % nsub) * 512:(k % nsub + 1) * 512]
        NB = len(slots["bufs"])
        LA = NB - 1
        for k in range(n + LA + 3):
            if k < n:
                _dmas(P, slots["bufs"][k % NB], stile(k), ["sync", "sync"],
                      tag="pl16")
            j = k - LA
            if 0 <= j < n:
                _copy(P, slots["q16"][j % 6], slots["bufs"][j % NB],
                      eng="vector")
            j3 = k - LA - 3
            if 0 <= j3 < n:
                _dma(P, dtile(j3), slots["q16"][j3 % 6], tag="pl16o",
                     eng="scalar")
            yield
        P.step("gpsimd", lambda e: e.collective_compute(
            "AllGather", ALU.bypass, replica_groups=[list(range(NCORES))],
            ins=[qsh_h.opt()], outs=[agout_h.opt()]),
            r=[qsh_h], w=[agout_h], cc=True, tag="plag")
        yield

    MWIX = dict(g0=0, u0=1, d0=2, g1=3, u1=4, d1=5)

    def prep_stats(src_h, nrowsh, ncol, mwslot, tag, slots, first=False):
        nb = nrowsh // 128
        nsub = ncol // 512
        n = nb * nsub
        accs = [col(PP_AC0), col(PP_AC1)]
        parts = [col(PP_PT0), col(PP_PT1), col(PP_PT2)]
        stile = lambda k: src_h[(k // nsub) * 128:(k // nsub + 1) * 128,
                                (k % nsub) * 512:(k % nsub + 1) * 512]
        if first:
            zr = slots["mwrt"]
            _memset(P, zr, 0.0)
            _dma(P, mw_in[0:1, 0:128], zr, eng="gpsimd", tag="mwz")
            yield

        def acc_(k):
            j = k % 2
            if k < 2:
                _copy(P, accs[j], parts[k % 3], eng="gpsimd")
            else:
                P.step("gpsimd", lambda e, o=accs[j], i=parts[k % 3]:
                       e.tensor_tensor(out=o, in0=o, in1=i, op=ALU.add),
                       r=[accs[j], parts[k % 3]], w=[accs[j]],
                       tag=tag + "acc")

        NB = len(slots["bufs"])
        LA = NB - 1
        for k in range(n + LA + 2):
            if k < n:
                _dmas(P, slots["bufs"][k % NB], stile(k), ["sync", "sync"],
                      tag=tag + "s")
            j = k - LA
            if 0 <= j < n:
                _red(P, parts[j % 3], slots["bufs"][j % NB],
                     ALU.add, absval=True)
            if 0 <= k - LA - 2 < n:
                acc_(k - LA - 2)
            yield
        P.step("gpsimd", lambda e, o=accs[0], i=accs[1]:
               e.tensor_tensor(out=o, in0=o, in1=i, op=ALU.add),
               r=[accs[0], accs[1]], w=[accs[0]], tag=tag + "accf")
        ix = MWIX[mwslot]
        mwrow = rows_h[9:10, 0:128]
        _dma(P, mwrow.rearrange("r (b p) -> p r b", p=128), accs[0],
             eng="gpsimd", tag=tag + "mw1")
        rtmp, mwv = slots["mwrt"], slots["mwv"]
        _dma(P, rtmp, mwrow, eng="gpsimd", tag=tag + "mw2")
        _red(P, mwv, rtmp, ALU.add)
        _dma(P, mw_in[0:1, ix:ix + 1], mwv, eng="gpsimd", tag=tag + "mw3")
        yield

    def mw_allreduce():
        P.step("gpsimd", lambda e: e.collective_compute(
            "AllReduce", ALU.add, replica_groups=[list(range(NCORES))],
            ins=[mw_in.opt()], outs=[mw_out.opt()]),
            r=[mw_in], w=[mw_out], cc=True, tag="mwar")
        yield

    def prep_quant(src_h, qsh_h, agout_h, nrow_full, ncol, mwslot, tag,
                   slots):
        nb = (nrow_full // NCORES) // 128
        nsub = ncol // 512
        n = nb * nsub
        ix = MWIX[mwslot]
        rtmp, mwv = slots["mwrt"], slots["mwv"]
        _dma(P, mwv, mw_out[0:1, ix:ix + 1], eng="gpsimd", tag=tag + "mwl")
        _ts2(P, mwv, mwv, f32(1.0 / (nrow_full * ncol)), f32(1e-5),
             ALU.mult, ALU.max)
        _ts1(P, rtmp, onesr32, mwv, ALU.mult)
        mwrow = rows_h[9:10, 0:128]
        _dma(P, mwrow, rtmp, eng="gpsimd", tag=tag + "mw4")
        mw = col(MW[mwslot])
        _dma(P, mw, mwrow.rearrange("r (p o) -> p (r o)", o=1),
             eng="gpsimd", tag=tag + "mw5")
        scw, half = col(PP_SCW), col(PP_HALF)
        _recip(P, scw, mw)
        _ts1(P, half, mw, f32(0.5), ALU.mult)
        yield

        stile = lambda k: src_h[(k // nsub) * 128:(k // nsub + 1) * 128,
                                (k % nsub) * 512:(k % nsub + 1) * 512]

        def signs_(k):
            buf = slots["bufs"][k % len(slots["bufs"])]
            sA, sB = slots["sA"][k % 3], slots["sB"][k % 3]
            q16 = slots["q16"][k % 6]
            _act(P, sA, buf, ACT.Sign, bias=f32(-0.5), scale=scw, tag="sgA")
            _act(P, sB, buf, ACT.Sign, bias=f32(0.5), scale=scw, tag="sgB")
            aeng = "vector" if k % 2 == 0 else "gpsimd"
            P.step(aeng, lambda e, o=sA, i=sB:
                   e.tensor_tensor(out=o, in0=o, in1=i, op=ALU.add),
                   r=[sA, sB], w=[sA], tag="wqadd")
            P.step("vector", lambda e, o=q16, i=sA, h_=half:
                   e.tensor_single_scalar(out=o, in_=i, scalar=h_,
                                          op=ALU.mult),
                   r=[sA, half], w=[q16], tag="wq16")

        NB = len(slots["bufs"])
        LA = NB - 1
        for k in range(n + LA + 3):
            if k < n:
                _dmas(P, slots["bufs"][k % NB], stile(k), ["sync", "sync"],
                      tag=tag + "q")
            j = k - LA
            if 0 <= j < n:
                signs_(j)
            j3 = k - LA - 3
            if 0 <= j3 < n:
                _dma(P, qsh_h[(j3 // nsub) * 128:(j3 // nsub + 1) * 128,
                              (j3 % nsub) * 512:(j3 % nsub + 1) * 512],
                     slots["q16"][j3 % 6], tag=tag + "o", eng="scalar")
            yield
        P.step("gpsimd", lambda e: e.collective_compute(
            "AllGather", ALU.bypass, replica_groups=[list(range(NCORES))],
            ins=[qsh_h.opt()], outs=[agout_h.opt()]),
            r=[qsh_h], w=[agout_h], cc=True, tag=tag + "ag")
        yield

    pending.append(prep_qkv16())
    for tt in range(T // 128):
        lb = big[:, (tt % 2) * D:(tt % 2) * D + D]
        _dmas(P, lb, xs_h[tt * 128:(tt + 1) * 128, :], ["sync", "sync"],
              tag="xld")
        for g in range(G):
            pt = ps[7][:, 0:128]
            _tp(P, pt, lb[:, g * 128:(g + 1) * 128], eye32[:], tag="xtp")
            _copy(P, x[:, g * T + tt * 128: g * T + (tt + 1) * 128], pt)
        pump(2)
    pump(64)

    def ln_slice(s, rows_m, rows_r, rows_b, rows_a, rows_si):
        s0 = s * TAU
        rsl = slice(s0, s0 + TAU)
        xb16 = [bb[:, A_Q2 + g * TAU:A_Q2 + (g + 1) * TAU] for g in range(G)]
        sq16 = [bb[:, A_K2 + g * TAU:A_K2 + (g + 1) * TAU] for g in range(G)]
        for g in range(G):
            xg = xg_sl(g, s0, TAU)
            _copy(P, xb16[g], xg, eng="scalar", tag="xb16")
            _act(P, sq16[g], xg, ACT.Square, tag="xsq")
        pS1, pS2 = ps[0][:, 0:TAU], ps[1][:, 0:TAU]
        for g in range(G):
            _mm(P, pS1, ones16[:], xb16[g], g == 0, g == G - 1, tag="lnS1")
        for g in range(G):
            _mm(P, pS2, ones16[:], sq16[g], g == 0, g == G - 1, tag="lnS2")
        m_t, r_t, t3, t4 = M_T(0), M_T(1), M_T(2), M_T(3)
        _ts1(P, m_t, pS1, f32(1.0 / D), ALU.mult, tag="ln_m")
        _act(P, t3, pS1, ACT.Square, scale=f32(1.0 / D), tag="ln_msq")
        _ts2(P, r_t, pS2, f32(1.0 / D), f32(1e-5), ALU.mult, ALU.add)
        _tt(P, r_t, r_t, t3, ALU.subtract)
        _act(P, r_t, r_t, ACT.Sqrt)
        _recip(P, r_t, r_t)
        amax = t3
        for g in range(G):
            xg = xg_sl(g, s0, TAU)
            _tt(P, t4, xg, m_t, ALU.subtract, tag="am_d")
            _act(P, t4, t4, ACT.Abs, tag="am_abs")
            if g == 0:
                _copy(P, amax, t4, eng="vector", tag="am0")
            else:
                _tt(P, amax, amax, t4, ALU.max, tag="am")
        r_am = R_S(0)
        pmax_to_row(amax, TAU, r_am, 2)
        r0, r1 = R_S(0), R_S(1)
        _tt(P, r0, r_am, r_t[0:1, :], ALU.mult, tag="mxrow")
        _ts1(P, r1, r0, f32(1e-5), ALU.max)
        _recip(P, r0, r1)
        _ts1(P, r0, r0, f32(127.0), ALU.mult)
        _tt(P, r0, r_t[0:1, :], r0, ALU.mult)
        _dma(P, rows_h[rows_a:rows_a + 1, rsl], r0, eng="gpsimd")
        _tt(P, r0, m_t[0:1, :], r0, ALU.mult)
        _dma(P, rows_h[rows_b:rows_b + 1, rsl], r0, eng="gpsimd")
        _ts1(P, r1, r1, f32(1.0 / 127.0), ALU.mult)
        _dma(P, rows_h[rows_si:rows_si + 1, rsl], r1, eng="gpsimd")
        return m_t, r_t

    ROW_B0, ROW_A0, ROW_SI0 = 0, 1, 2
    ROW_M1, ROW_R1, ROW_B1, ROW_A1, ROW_SI1 = 3, 4, 5, 6, 7
    SP = cfg.get("stop_phase", 99)

    kvacc = lambda cg: big[:, B_KV + cg * D: B_KV + (cg + 1) * D]
    vstage = lambda g: big[:, 8192 + g * TAU:8192 + (g + 1) * TAU]
    kstage = lambda g: big[:, 8192 + (G + g) * TAU:8192 + (G + g + 1) * TAU]
    AP_ = cfg.get("a_parts", 99)
    slotsA["mwrt"] = msc[0:1, 3 * TAU:3 * TAU + 128]
    slotsA["mwv"] = msc[0:1, 2 * TAU:2 * TAU + 1]
    if SP >= 4:
        pending.append(prep_stats(g0s_h, SHW, INNER, "g0", "g0", slotsA,
                                  first=True))
        pending.append(prep_stats(u0s_h, SHW, INNER, "u0", "u0", slotsA))
        pending.append(prep_stats(d0s_h, SHD, D, "d0", "d0", slotsA))
    if SP >= 8:
        pending.append(prep_stats(g1s_h, SHW, INNER, "g1", "g1", slotsA))
        pending.append(prep_stats(u1s_h, SHW, INNER, "u1", "u1", slotsA))
        pending.append(prep_stats(d1s_h, SHD, D, "d1", "d1", slotsA))
    if SP >= 4:
        pending.append(mw_allreduce())
        pending.append(prep_quant(g0s_h, gq_s[0], gq_h[0], D, INNER,
                                  "g0", "g0", slotsA))
        pending.append(prep_quant(u0s_h, uq_s[0], uq_h[0], D, INNER,
                                  "u0", "u0", slotsA))
        pending.append(prep_quant(d0s_h, dq_s[0], dq_h[0], INNER, D,
                                  "d0", "d0", slotsA))
    if SP >= 6:
        pending.append(prep_plain16(wlns_h, wln_s16, wln16_h, SHW, slotsA))
    for s in range(NS if SP >= 2 else 0):
        s0 = s * TAU
        sl = slice(s0, s0 + TAU)
        m_t, r_t = ln_slice(s, None, None, ROW_B0, ROW_A0, ROW_SI0)
        pump(4)
        if AP_ < 2:
            continue
        t4 = M_T(3)
        xn16 = []
        for g in range(G):
            xg = xg_sl(g, s0, TAU)
            _tt(P, t4, xg, m_t, ALU.subtract, tag="xn_s")
            dst = bb[:, A_XN + g * TAU:A_XN + (g + 1) * TAU]
            _tt(P, dst, t4, r_t, ALU.mult, tag="xn_m")
            xn16.append(dst)
        if AP_ < 3:
            continue
        q2 = [bb[:, A_Q2 + g * TAU:A_Q2 + (g + 1) * TAU] for g in range(G)]
        k2 = [bb[:, A_K2 + g * TAU:A_K2 + (g + 1) * TAU] for g in range(G)]
        v16 = [bb[:, A_V + g * TAU:A_V + (g + 1) * TAU] for g in range(G)]
        OB = 3
        for oblk in range(24 // OB):
            w0 = W_QKV + (oblk % 2) * (OB * 128 * G)
            wb = wst[:, w0:w0 + OB * 128 * G]
            src3 = qkv16_h.rearrange("(cg p) o -> p cg o", p=128)[
                :, :, oblk * OB * 128:(oblk + 1) * OB * 128]
            _dmas(P, wb.rearrange("p (cg o) -> p cg o", cg=G), src3,
                  ["sync", "sync", "sync", "sync"], tag="qkvwld")
            for o in range(OB):
                oo = oblk * OB + o
                nmm = TAU // 512 if TAU > 512 else 1
                pb = ps[oo % 6][:, 0:TAU]
                for cg in range(G):
                    lt = wb[:, cg * OB * 128 + o * 128:
                            cg * OB * 128 + (o + 1) * 128]
                    _mm(P, pb, lt, xn16[cg], cg == 0, cg == G - 1, tag="qkv")
                g = oo % G
                if oo < 8:
                    _act(P, q2[g], pb, ACT.Square, bias=col(C_BQ + g), tag="q2")
                elif oo < 16:
                    _act(P, k2[g], pb, ACT.Square, bias=col(C_BK + g), tag="k2")
                else:
                    _act(P, vstage(g), pb, ACT.Identity, bias=col(C_BV + g),
                         tag="v32")
                    _copy(P, v16[g], vstage(g), eng="vector", tag="v16")
                pump(4)
            pump(4)
        if AP_ < 4:
            continue
        for g in range(G):
            _dma(P, vs_h[g * 128:(g + 1) * 128, sl], v16[g], eng="gpsimd",
                 tag="vst")
        sq = [bb[:, A_XN + g * TAU:A_XN + (g + 1) * TAU] for g in range(G)]
        pQ2, pQQ, pK2 = ps[6][:, 0:TAU], ps[7][:, 0:TAU], ps[0][:, 0:TAU]
        for g in range(G):
            _mm(P, pQ2, ones16[:], q2[g], g == 0, g == G - 1, tag="S_q2")
        for g in range(G):
            _act(P, sq[g], q2[g], ACT.Square, tag="q2sq")
        for g in range(G):
            _mm(P, pQQ, ones16[:], sq[g], g == 0, g == G - 1, tag="S_qq")
        for g in range(G):
            _mm(P, pK2, ones16[:], k2[g], g == 0, g == G - 1, tag="S_k2")
        dq_t, rk_t, t3 = M_T(0), M_T(1), M_T(2)
        _act(P, dq_t, pQQ, ACT.Sqrt, tag="r2")
        _ts2(P, t3, pQ2, f32(1e-12), f32(1e-19), ALU.mult, ALU.add)
        _tt(P, dq_t, dq_t, t3, ALU.max)
        _recip(P, dq_t, dq_t)
        _ts1(P, rk_t, pK2, f32(1e-7), ALU.add)
        _recip(P, rk_t, rk_t)
        for g in range(G):
            _tt(P, q2[g], q2[g], dq_t, ALU.mult, tag="q2b")
            _dma(P, q2b_h[g * 128:(g + 1) * 128, sl], q2[g], eng="gpsimd",
                 tag="q2bst")
            _tt(P, kstage(g), k2[g], rk_t, ALU.mult, tag="k2a")
        for g in range(G):
            _act(P, sq[g], kstage(g), ACT.Square, tag="k2asq")
            if s == 0:
                _red(P, col(C_CSQ + g), sq[g], ALU.add, tag="csq0")
            else:
                _red(P, col(T4), sq[g], ALU.add, tag="csq")
                _tt(P, col(C_CSQ + g), col(C_CSQ + g), col(T4), ALU.add)
        pump(6)
        if AP_ < 5:
            continue
        for tb in range(TAUB):
            tb0 = W_TB
            kT = wst[:, tb0:tb0 + 1024]
            vT = wst[:, tb0 + 1024:tb0 + 2048]
            for g in range(G):
                pt = ps[1][:, 0:128]
                _tp(P, pt, kstage(g)[:, tb * 128:(tb + 1) * 128], eye32[:],
                    tag="k2aT")
                _copy(P, kT[:, g * 128:(g + 1) * 128], pt)
                pt2 = ps[2][:, 0:128]
                _tp(P, pt2, vstage(g)[:, tb * 128:(tb + 1) * 128], eye32[:],
                    tag="vT")
                _copy(P, vT[:, g * 128:(g + 1) * 128], pt2)
                pump(4)
            glob_tb = s * TAUB + tb
            for cg in range(G):
                for half in range(2):
                    pkv = ps[3 + half][:, 0:512]
                    _mm(P, pkv, kT[:, cg * 128:(cg + 1) * 128],
                        vT[:, half * 512:(half + 1) * 512], True, True,
                        tag="kvmm")
                    dst = kvacc(cg)[:, half * 512:(half + 1) * 512]
                    if glob_tb == 0:
                        _copy(P, dst, pkv, tag="kva0")
                    else:
                        _tt(P, dst, dst, pkv, ALU.add, tag="kva")
            pump(9)

    drain()
    for cg in range(G if SP >= 3 else 0):
        _dma(P, cc_in[cg * 128:(cg + 1) * 128, :], kvacc(cg), eng="gpsimd",
             tag="ccin")
    _dma(P, cc_in[D:D + 1, :].rearrange("r (g p) -> p r g", p=128),
         col(C_CSQ, G), eng="gpsimd", tag="ccin2")
    if SP >= 3:
        P.step("gpsimd", lambda e: e.collective_compute(
            "AllReduce", ALU.add, replica_groups=[list(range(NCORES))],
            ins=[cc_in.opt()], outs=[cc_out.opt()]),
            r=[cc_in], w=[cc_out], cc=True, tag="allreduce")

    if SP >= 8:
        pending.append(prep_quant(g1s_h, gq_s[1], gq_h[1], D, INNER,
                                  "g1", "g1", slotsF))
        pending.append(prep_quant(u1s_h, uq_s[1], uq_h[1], D, INNER,
                                  "u1", "u1", slotsF))
        pending.append(prep_quant(d1s_h, dq_s[1], dq_h[1], INNER, D,
                                  "d1", "d1", slotsF))
    if SP >= 9:
        pending.append(prep_plain16(l1s_h, l1_s16, l1w16_h, 4 * SHW, slotsF))

    def ffn(gqh, uqh, dqh, mwd_col, rb, ra, rsi, dbgname=None):
        for s in range(NS):
            s0 = s * TAU
            a_t, b_t, si_t = M_T(0), M_T(1), M_T(2)
            w512 = min(512, TAU)
            for (ri, tt_) in ((ra, a_t), (rb, b_t), (rsi, si_t)):
                rr = R_S(0)
                _dma(P, rr, rows_h[ri:ri + 1, s0:s0 + TAU], tag="rowld")
                for b_ in range(TAU // w512):
                    pb = ps[0][:, 0:w512]
                    _mm(P, pb, onesr32,
                        rr[:, b_ * w512:(b_ + 1) * w512], True, True,
                        tag="rbc")
                    _copy(P, tt_[:, b_ * w512:(b_ + 1) * w512], pb)
            zbs = [S1[:, 0:TAU], pstg[:, 0:TAU]]
            r1fs = [S2[:, 0:TAU], pstg[:, TAU:2 * TAU]]
            r1i = I1[:, 0:TAU]
            xq16 = [bb[:, F_XQ + g * TAU:F_XQ + (g + 1) * TAU]
                    for g in range(G)]
            for g in range(G + 1):
                if g < G:
                    zb = zbs[g % 2]
                    xg = xg_sl(g, s0, TAU)
                    _tt(P, zb, xg, a_t, ALU.mult, tag="f_z")
                    P.step("gpsimd", lambda e, o=zb, i=zb, b=b_t:
                           e.tensor_tensor(out=o, in0=i, in1=b,
                                           op=ALU.subtract),
                           r=[zb, b_t], w=[zb], tag="f_z2")
                    _act(P, r1i, zb, ACT.Identity, bias=f32(512.0),
                         tag="f_cvt")
                    _copy(P, r1fs[g % 2], r1i, eng="scalar", tag="f_cvb")
                if g >= 1:
                    zb, r1f = zbs[(g - 1) % 2], r1fs[(g - 1) % 2]
                    _stt(P, zb, zb, 512.0, r1f, ALU.add, ALU.subtract,
                         tag="f_d")
                    _stt(P, zb, zb, 0.5, r1f, ALU.is_ge, ALU.add, tag="f_rq")
                    _stt(P, xq16[g - 1], zb, f32(-512.0), si_t, ALU.add,
                         ALU.mult, tag="xq16")
                    if cfg["debug"] and dbgname is not None:
                        _dma(P, dbg[dbgname][(g - 1) * 128:g * 128,
                                             s0:s0 + TAU],
                             xq16[g - 1], eng="gpsimd", tag="dbgxq")
            NIB = 4
            h_of = lambda i: big[:, B_H + i * TAU:B_H + (i + 1) * TAU]
            amax, gs = M_T(3), S1[:, 0:TAU]
            mneg = S2[:, 0:TAU]
            for ib in range(IG // NIB):
                gwb = wst[:, WF_G + (ib % 2) * 4096:
                          WF_G + (ib % 2) * 4096 + 4096]
                uwb = wst[:, WF_U:WF_U + 4096]
                gsrc = gqh.rearrange("(cg p) i -> p cg i", p=128)[
                    :, :, ib * NIB * 128:(ib + 1) * NIB * 128]
                usrc = uqh.rearrange("(cg p) i -> p cg i", p=128)[
                    :, :, ib * NIB * 128:(ib + 1) * NIB * 128]
                _dmas(P, gwb.rearrange("p (cg i) -> p cg i", cg=G), gsrc,
                      ["sync", "sync", "sync", "sync"], tag="gld")
                _dmas(P, uwb.rearrange("p (cg i) -> p cg i", cg=G), usrc,
                      ["gpsimd", "gpsimd", "gpsimd", "gpsimd"], tag="uld")
                pump(2)
                for ii in range(NIB):
                    i = ib * NIB + ii
                    pg = ps[ii][:, 0:TAU]
                    pu = ps[4 + ii][:, 0:TAU]
                    for cg in range(G):
                        lt = gwb[:, cg * NIB * 128 + ii * 128:
                                 cg * NIB * 128 + (ii + 1) * 128]
                        _mm(P, pg, lt, xq16[cg], cg == 0, cg == G - 1,
                            tag="gmm")
                    for cg in range(G):
                        lt = uwb[:, cg * NIB * 128 + ii * 128:
                                 cg * NIB * 128 + (ii + 1) * 128]
                        _mm(P, pu, lt, xq16[cg], cg == 0, cg == G - 1,
                            tag="umm")
                    _act(P, gs, pg, ACT.Silu, tag="silu")
                    _tt(P, h_of(i), gs, pu, ALU.mult, tag="h")
                    if i == 0:
                        _copy(P, amax, h_of(i), eng="vector", tag="am0")
                        _ts1(P, mneg, h_of(i), -1.0, ALU.mult, tag="amn0")
                    else:
                        _tt(P, amax, amax, h_of(i), ALU.max, tag="am")
                        _stt(P, mneg, h_of(i), -1.0, mneg, ALU.mult, ALU.max,
                             tag="amn")
                    pump(3)
            drain()
            _tt(P, amax, amax, mneg, ALU.max, tag="am_fin")
            r_am2 = R_S(0)
            pmax_to_row(amax, TAU, r_am2, 0)
            rc, rs = R_S(0), R_S(1)
            _ts1(P, rc, r_am2, f32(1e-5), ALU.max)
            _recip(P, rs, rc)
            _ts1(P, rs, rs, f32(127.0), ALU.mult)
            _ts1(P, rc, rc, f32(1.0 / 127.0), ALU.mult)
            sc2_t, ds2_t = M_T(2), M_T(3)
            for (rr, tt_) in ((rs, sc2_t), (rc, ds2_t)):
                for b_ in range(TAU // w512):
                    pb = ps[0][:, 0:w512]
                    _mm(P, pb, onesr32,
                        rr[:, b_ * w512:(b_ + 1) * w512], True, True,
                        tag="r2bc")
                    _copy(P, tt_[:, b_ * w512:(b_ + 1) * w512], pb)
            z2s = [M_T(0), M_T(1), pstg[:, 0:TAU]]
            r2fs = [S1[:, 0:TAU], S2[:, 0:TAU], pstg[:, TAU:2 * TAU]]

            def d_stage1(i):
                par = i % 3
                z2 = z2s[par]
                r2i = I1[:, 0:TAU]
                P.step("gpsimd", lambda e, o=z2, h=h_of(i), s=sc2_t:
                       e.tensor_tensor(out=o, in0=h, in1=s, op=ALU.mult),
                       r=[h_of(i), sc2_t], w=[z2], tag="z2")
                _act(P, r2i, z2, ACT.Identity, bias=f32(512.0), tag="d_cvt")
                _copy(P, r2fs[par], r2i, eng="scalar", tag="d_cvb")

            def d_stage2(i):
                par = i % 3
                z2, r2f = z2s[par], r2fs[par]
                _stt(P, z2, z2, 512.0, r2f, ALU.add, ALU.subtract, tag="d_d")
                _stt(P, z2, z2, 0.5, r2f, ALU.is_ge, ALU.add, tag="d_rq")
                xq2 = bb[:, F_XQ2 + par * TAU:F_XQ2 + par * TAU + TAU]
                _ts1(P, xq2, z2, f32(-512.0), ALU.add, tag="xq2")

            def d_mm(i):
                xq2 = bb[:, F_XQ2 + (i % 3) * TAU:F_XQ2 + (i % 3) * TAU + TAU]
                dwb = wst[:, WF_D + (i % 2) * D:WF_D + (i % 2) * D + D]
                _dmas(P, dwb, dqh[i * 128:(i + 1) * 128, :],
                      ["sync", "gpsimd"], tag="dld")
                for og in range(G):
                    _mm(P, ps[og][:, 0:TAU],
                        dwb[:, og * 128:(og + 1) * 128], xq2,
                        i == 0, i == IG - 1, tag="dmm")

            for i in range(IG + 3):
                if i < IG:
                    d_stage1(i)
                if 1 <= i <= IG:
                    d_stage2(i - 1)
                if i >= 3:
                    d_mm(i - 3)
            for og in range(G):
                ztmp = z2s[og % 2]
                _tt(P, ztmp, ps[og][:, 0:TAU], ds2_t, ALU.mult, tag="d_ds")
                xg = xg_sl(og, s0, TAU)
                _tt(P, xg, xg, ztmp, ALU.add, tag="d_add")

    if SP >= 5:
        ffn(gq_h[0], uq_h[0], dq_h[0], col(MW["d0"]), ROW_B0, ROW_A0,
            ROW_SI0, dbgname="xq" if cfg["debug"] else None)
    drain()

    if cfg["debug"]:
        for g in range(G):
            nch = min(T, 4 * D)
            b = big[:, 0:nch]
            _copy(P, b, x[:, g * T:g * T + nch], eng="vector")
            _dma(P, dbg["xf"][g * 128:(g + 1) * 128, 0:nch], b, eng="gpsimd")

    wln16 = wst[:, WB_LN:WB_LN + G * D]

    def kv16s(cg):
        if cg < 4:
            return wst[:, 8192 + cg * D:8192 + (cg + 1) * D]
        return bb[:, 12288 + (cg - 4) * D:12288 + (cg - 3) * D]

    dcol, csq = col(T5), col(T4)
    for cg in range(G if SP >= 6 else 0):
        kbuf = big[:, 0:D]
        _dma(P, kbuf, cc_out[cg * 128:(cg + 1) * 128, :], tag="kvld")
        _dma(P, csq, cc_out[D:D + 1, cg * 128:(cg + 1) * 128].rearrange(
            "r p -> p r"), tag="csqld")
        _act(P, dcol, csq, ACT.Sqrt)
        _ts1(P, dcol, dcol, f32(1e-12), ALU.max)
        _recip(P, dcol, dcol)
        _ts1(P, dcol, dcol, f32(1.0 / RT), ALU.mult)
        _ts1(P, kv16s(cg), kbuf, dcol, ALU.mult, tag="kv16")
    for cg in range(G if SP >= 6 else 0):
        _dma(P, wln16[:, cg * D:(cg + 1) * D],
             wln16_h[cg * 128:(cg + 1) * 128, :], tag="wlnld")
    if cfg["debug"]:
        for cg in range(G):
            kb = big[:, D:2 * D]
            _copy(P, kb, kv16s(cg), eng="vector")
            _dma(P, dbg["kv"][cg * 128:(cg + 1) * 128, :], kb, eng="gpsimd")

    for s in range(NS if SP >= 6 else 0):
        s0 = s * TAU
        sl = slice(s0, s0 + TAU)
        q2r = [bb[:, B_Q2R + g * TAU:B_Q2R + (g + 1) * TAU] for g in range(G)]
        vr = [bb[:, B_VR + g * TAU:B_VR + (g + 1) * TAU] for g in range(G)]
        vt = [bb[:, B_VT + g * TAU:B_VT + (g + 1) * TAU] for g in range(G)]
        for g in range(G):
            _dma(P, q2r[g], q2b_h[g * 128:(g + 1) * 128, sl], tag="q2bld")
            _dma(P, vr[g], vs_h[g * 128:(g + 1) * 128, sl], eng="gpsimd",
                 tag="vld")
        for dg in range(G):
            pb = ps[dg % 4][:, 0:TAU]
            for cg in range(G):
                _mm(P, pb, kv16s(cg)[:, dg * 128:(dg + 1) * 128],
                    q2r[cg], cg == 0, cg == G - 1, tag="t2mm")
            _tt(P, vt[dg], vr[dg], pb, ALU.add, tag="vt")
        for eg in range(G):
            pb = ps[4 + eg % 4][:, 0:TAU]
            for dg in range(G):
                _mm(P, pb,
                    wln16[:, dg * D + eg * 128:dg * D + (eg + 1) * 128],
                    vt[dg], dg == 0, dg == G - 1, tag="lnmm")
            xg = xg_sl(eg, s0, TAU)
            _stt(P, xg, pb, col(C_BLN + eg), xg, ALU.add, ALU.add,
                 tag="attn0+")

    if cfg["debug"]:
        for g in range(G):
            nch = min(T, IG * TAU)
            b = big[:, 0:nch]
            _copy(P, b, x[:, g * T:g * T + nch], eng="vector")
            _dma(P, dbg["x1"][g * 128:(g + 1) * 128, 0:nch], b, eng="gpsimd")

    for s in range(NS if SP >= 7 else 0):
        s0 = s * TAU
        sl = slice(s0, s0 + TAU)
        m_t, r_t = ln_slice(s, None, None, ROW_B1, ROW_A1, ROW_SI1)
        t4 = M_T(3)
        for g in range(G):
            xg = xg_sl(g, s0, TAU)
            _tt(P, t4, xg, m_t, ALU.subtract, tag="xn1_s")
            dst = bb[:, A_XN + g * TAU:A_XN + (g + 1) * TAU]
            _tt(P, dst, t4, r_t, ALU.mult, tag="xn1_m")
            _dma(P, xn1_h[g * 128:(g + 1) * 128, sl], dst, eng="gpsimd",
                 tag="xn1st")
    if SP >= 7:
        for g in range(G):
            _dma(P, ag_in[g * 128:(g + 1) * 128, :],
                 x[:, g * T + T - WP: g * T + T], eng="gpsimd", tag="agin")
        P.step("gpsimd", lambda e: e.collective_compute(
            "AllGather", ALU.bypass, replica_groups=[list(range(NCORES))],
            ins=[ag_in.opt()], outs=[ag_out.opt()]),
            r=[ag_in], w=[ag_out], cc=True, tag="allgather")

    if SP >= 8:
        ffn(gq_h[1], uq_h[1], dq_h[1], col(MW["d1"]), ROW_B1, ROW_A1,
            ROW_SI1)
    drain()

    xw = [big[:, B_XW + g * WP:B_XW + (g + 1) * WP] for g in range(G)]
    cand = big[:, B_CAND:B_CAND + WP]
    for g in range(G if SP >= 9 else 0):
        for j in range(NCORES):
            _dma(P, cand, ag_out[j * D + g * 128:j * D + (g + 1) * 128, :],
                 tag="agc")
            if j == 0:
                _ts1(P, xw[g], cand, col(C_OH + j), ALU.mult, tag="xw0")
            else:
                _stt(P, xw[g], cand, col(C_OH + j), xw[g], ALU.mult, ALU.add,
                     tag="xw+")

    xn1w = [bb[:, L_XN1 + g * WP:L_XN1 + (g + 1) * WP] for g in range(G)]
    xb16w = [bb[:, L_DX + g * WP:L_DX + (g + 1) * WP] for g in range(G)]
    sq16w = [bb[:, L_XV + g * WP:L_XV + (g + 1) * WP] for g in range(G)]
    if SP >= 9:
        pS1, pS2 = ps[0][:, 0:WP], ps[1][:, 0:WP]
        for g in range(G):
            _copy(P, xb16w[g], xw[g], eng="vector")
            _act(P, sq16w[g], xw[g], ACT.Square)
        for g in range(G):
            _mm(P, pS1, ones16[:], xb16w[g], g == 0, g == G - 1, tag="wS1")
        for g in range(G):
            _mm(P, pS2, ones16[:], sq16w[g], g == 0, g == G - 1, tag="wS2")
        m_t = msc[:, 0:WP]
        r_t = msc[:, WP:2 * WP]
        t3 = msc[:, 2 * WP:3 * WP]
        _ts1(P, m_t, pS1, f32(1.0 / D), ALU.mult)
        _act(P, t3, pS1, ACT.Square, scale=f32(1.0 / D))
        _ts2(P, r_t, pS2, f32(1.0 / D), f32(1e-5), ALU.mult, ALU.add)
        _tt(P, r_t, r_t, t3, ALU.subtract)
        _act(P, r_t, r_t, ACT.Sqrt)
        _recip(P, r_t, r_t)
        for g in range(G):
            _tt(P, t3, xw[g], m_t, ALU.subtract)
            _tt(P, xn1w[g], t3, r_t, ALU.mult, tag="xn1w")

    e_b = lambda g: big[:, B_E + g * SL:B_E + (g + 1) * SL]
    ev_b = lambda g: big[:, B_EV + g * SL:B_EV + (g + 1) * SL]
    Ab = lambda g: big[:, B_AB + g * (SL + 1):B_AB + (g + 1) * (SL + 1)]
    Bb = lambda g: big[:, B_BB + g * (SL + 1):B_BB + (g + 1) * (SL + 1)]
    for g in range(G if SP >= 10 else 0):
        _memset(P, Ab(g)[:, 0:1], 0.0)
        _memset(P, Bb(g)[:, 0:1], 0.0)

    kw = wst[:, WL_K:WL_K + 4096]
    vw = wst[:, WL_V:WL_V + 4096]
    ow = wst[:, WL_O:WL_O + 4096]

    def l1_proj(psums, wbuf, wrow0, xin, n, tag):
        for half in range(2):
            src = l1w16_h[wrow0 + half * 512:wrow0 + (half + 1) * 512, :]
            src = src.rearrange("(c p) e -> p c e", p=128)
            _dma(P, wbuf.rearrange("p (c e) -> p c e", c=4), src,
                 tag=tag + "w")
            for eg in range(G):
                for dc in range(4):
                    d = half * 4 + dc
                    lt = wbuf[:, dc * D + eg * 128:dc * D + (eg + 1) * 128]
                    _mm(P, psums[eg][:, 0:n], lt, xin[d], d == 0, d == 7,
                        tag=tag)

    total_sl = NWSL + NSL
    for isl in range(total_sl if SP >= 10 else 0):
        warm = isl < NWSL
        n = SL
        if warm:
            xs_sl = [xn1w[g][:, isl * SL:isl * SL + n + 1] for g in range(G)]
        else:
            s_own = isl - NWSL
            tok0 = s_own * SL
            xn1sl = [bb[:, L_XN1 + g * (SL + 1):L_XN1 + (g + 1) * (SL + 1)]
                     for g in range(G)]
            for g in range(G):
                _dma(P, xn1sl[g][:, 1:n + 1],
                     xn1_h[g * 128:(g + 1) * 128, tok0:tok0 + n], tag="xn1ld")
                _copy(P, xn1sl[g][:, 0:1], col(C_PCOL + g), eng="vector")
            xs_sl = xn1sl
        for g in range(G):
            if warm and isl == NWSL - 1:
                _ts1(P, col(C_PCOL + g), xs_sl[g][:, n:n + 1], col(C_MASK),
                     ALU.mult, tag="pcolm")
            else:
                _copy(P, col(C_PCOL + g), xs_sl[g][:, n:n + 1], eng="vector")
        dx = [bb[:, L_DX + g * SL:L_DX + (g + 1) * SL] for g in range(G)]
        xk = [bb[:, L_XK + g * SL:L_XK + (g + 1) * SL] for g in range(G)]
        xv = [bb[:, L_XV + g * SL:L_XV + (g + 1) * SL] for g in range(G)]
        for g in range(G):
            _tt(P, dx[g], xs_sl[g][:, 1:n + 1], xs_sl[g][:, 0:n],
                ALU.subtract, tag="dx")
            _stt(P, xk[g], dx[g], col(C_MK + g), xs_sl[g][:, 0:n], ALU.mult,
                 ALU.add, tag="xk")
            _stt(P, xv[g], dx[g], col(C_MV + g), xs_sl[g][:, 0:n], ALU.mult,
                 ALU.add, tag="xv")
        l1_proj(ps, kw, 0, xk, n, "kp")
        for g in range(G):
            _act(P, e_b(g)[:, 0:n], ps[g][:, 0:n], ACT.Exp, tag="e")
        l1_proj(ps, vw, D, xv, n, "vp")
        for g in range(G):
            _tt(P, ev_b(g)[:, 0:n], e_b(g)[:, 0:n], ps[g][:, 0:n], ALU.mult,
                tag="ev")
        if warm:
            for g in range(G):
                _ts1(P, e_b(g)[:, 0:n], e_b(g)[:, 0:n], col(C_MASK),
                     ALU.mult, tag="emask")
                _ts1(P, ev_b(g)[:, 0:n], ev_b(g)[:, 0:n], col(C_MASK),
                     ALU.mult, tag="evmask")
        for g in range(G):
            _scan(P, Ab(g)[:, 1:n + 1],
                  col(C_EW + g).broadcast_to([128, n]), ev_b(g)[:, 0:n],
                  Ab(g)[:, 0:1], tag="scanA")
            _scan(P, Bb(g)[:, 1:n + 1],
                  col(C_EW + g).broadcast_to([128, n]), e_b(g)[:, 0:n],
                  Bb(g)[:, 0:1], tag="scanB")
        if not warm:
            xr = xk
            for g in range(G):
                _stt(P, xr[g], dx[g], col(C_MR + g), xs_sl[g][:, 0:n],
                     ALU.mult, ALU.add, tag="xr")
            for g in range(G):
                _stt(P, ev_b(g)[:, 0:n], ev_b(g)[:, 0:n], col(C_EU + g),
                     Ab(g)[:, 0:n], ALU.mult, ALU.add, tag="num")
                _stt(P, e_b(g)[:, 0:n], e_b(g)[:, 0:n], col(C_EU + g),
                     Bb(g)[:, 0:n], ALU.mult, ALU.add, tag="den")
            wkv = [bb[:, L_WKV + g * SL:L_WKV + (g + 1) * SL]
                   for g in range(G)]
            for g in range(G):
                _recip(P, e_b(g)[:, 0:n], e_b(g)[:, 0:n], tag="rden")
                _tt(P, wkv[g], ev_b(g)[:, 0:n], e_b(g)[:, 0:n], ALU.mult,
                    tag="wkv")
            l1_proj(ps, ow, 2 * D, xr, n, "rp")
            sr = [bb[:, L_SR + g * SL:L_SR + (g + 1) * SL] for g in range(G)]
            for g in range(G):
                _act(P, sr[g], ps[g][:, 0:n], ACT.Sigmoid, tag="sr")
                _tt(P, wkv[g], wkv[g], sr[g], ALU.mult, tag="rwkv")
            if cfg["debug"]:
                for g in range(G):
                    bscr = msc[:, 0:n]
                    _copy(P, bscr, wkv[g], eng="vector")
                    _dma(P, dbg["wkv"][g * 128:(g + 1) * 128,
                                       tok0:tok0 + n], bscr, eng="gpsimd")
            l1_proj(ps, ow, 3 * D, wkv, n, "op")
            for eg in range(G):
                xg = xg_sl(eg, tok0, n)
                _tt(P, xg, xg, ps[eg][:, 0:n], ALU.add, tag="l1+")
        for g in range(G):
            _copy(P, Ab(g)[:, 0:1], Ab(g)[:, n:n + 1], eng="vector")
            _copy(P, Bb(g)[:, 0:1], Bb(g)[:, n:n + 1], eng="vector")

    for tt in range(T // 128):
        ob = big[:, (tt % 2) * D:(tt % 2) * D + D]
        for g in range(G):
            pt = ps[tt % 4][:, 0:128]
            _tp(P, pt, x[:, g * T + tt * 128:g * T + (tt + 1) * 128],
                eye32[:], tag="otp")
            _copy(P, ob[:, g * 128:(g + 1) * 128], pt)
        _dma(P, out_h[tt * 128:(tt + 1) * 128, :], ob, tag="ost")

    if cfg["debug"]:
        _dma(P, dbg["gq"], gq_h[0], eng="gpsimd", tag="dbggq")
    P.step("gpsimd", lambda e: e.nop(), r=[out_h], tag="fence")

    stats = P.finalize(stack)
    stack.close()
    build_program.lastP = P
    return nc, stats


def _layout_inputs(inputs, cfg):
    T = cfg["T"]
    x = np.ascontiguousarray(np.asarray(inputs["L0_attn_x"],
                                        dtype=np.float32)).reshape(-1, D)
    A = lambda k: np.ascontiguousarray(np.asarray(inputs[k], np.float32))
    AT = lambda k: np.ascontiguousarray(np.asarray(inputs[k], np.float32).T)
    base = dict(
        wqkv=A("L0_attn_Wqkv"), bqkv=A("L0_attn_bqkv"),
        bln=A("L0_attn_bln"),
        td=A("L1_attn_time_decay"), tf=A("L1_attn_time_first"),
        mixk=A("L1_attn_time_mix_k").reshape(-1),
        mixv=A("L1_attn_time_mix_v").reshape(-1),
        mixr=A("L1_attn_time_mix_r").reshape(-1),
    )
    wln = A("L0_attn_Wln")
    g0t, u0t, d0t = AT("L0_ffn_w_gate"), AT("L0_ffn_w_up"), \
        AT("L0_ffn_w_down")
    g1t, u1t, d1t = AT("L1_ffn_w_gate"), AT("L1_ffn_w_up"), \
        AT("L1_ffn_w_down")
    l1full = np.vstack([AT("L1_attn_Wk"), AT("L1_attn_Wv"),
                        AT("L1_attn_Wr"), AT("L1_attn_Wo")])
    SHW, SHD = D // NCORES, (4 * D) // NCORES
    in_maps = []
    for c in range(NCORES):
        m = dict(base)
        m["xs"] = np.ascontiguousarray(x[c * T:(c + 1) * T, :])
        m["wmask"] = np.array([[0.0 if c == 0 else 1.0]], np.float32)
        oh = np.zeros((1, NCORES), np.float32)
        oh[0, (c - 1) % NCORES] = 1.0
        m["onehot"] = oh
        rw = slice(c * SHW, (c + 1) * SHW)
        rd = slice(c * SHD, (c + 1) * SHD)
        m["wlns"] = np.ascontiguousarray(wln[rw])
        m["g0s"] = np.ascontiguousarray(g0t[rw])
        m["u0s"] = np.ascontiguousarray(u0t[rw])
        m["d0s"] = np.ascontiguousarray(d0t[rd])
        m["g1s"] = np.ascontiguousarray(g1t[rw])
        m["u1s"] = np.ascontiguousarray(u1t[rw])
        m["d1s"] = np.ascontiguousarray(d1t[rd])
        m["l1s"] = np.ascontiguousarray(l1full[rd])
        in_maps.append(m)
    return in_maps


def kernel(**inputs):
    cfg = dict(FULL_CFG)
    nc, _ = build_program(cfg)
    in_maps = _layout_inputs(inputs, cfg)
    res = run_bass_kernel_spmd(nc, in_maps, core_ids=list(range(NCORES)))
    T = cfg["T"]
    out = np.concatenate([np.asarray(r["out"]) for r in res.results], axis=0)
    return np.ascontiguousarray(out.reshape(1, NCORES * T, D),
                                dtype=np.float32)
